# version 01
# baseline (speedup 1.0000x reference)
import math
from contextlib import ExitStack

import numpy as np
import concourse.bass as bass
import concourse.mybir as mybir
from concourse.bass_utils import run_bass_kernel_spmd

F32 = mybir.dt.float32
BF16 = mybir.dt.bfloat16
AF = mybir.ActivationFunctionType
ALU = mybir.AluOpType
AX = mybir.AxisListType.X

NEG = -3.0e38
ENGS = ("tensor", "vector", "scalar", "gpsimd", "sync")


class Cfg:
    def __init__(self, SEQ=8192, GROUP=4):
        self.SEQ = SEQ
        self.DM = 1024
        self.GROUP = GROUP
        self.OWN = SEQ // GROUP
        self.H = 16
        self.HD = 64
        self.TT = SEQ // 128
        self.OT = self.OWN // 128
        self.W512 = SEQ // 512
        self.OW = self.OWN // 512


CFG = Cfg()


class Sched:

    def __init__(self, nc, stack):
        self.nc = nc
        self.stack = stack
        self.q = {e: [] for e in ENGS}
        self.cnt = {}
        self.hwm = {e: {} for e in ENGS}
        self.sems = {}
        self.wmap = {}
        self.rmap = {}
        self.fence_pending = {e: [] for e in ENGS}

    def sem(self, name):
        if name not in self.sems:
            self.sems[name] = self.stack.enter_context(self.nc.semaphore(name))
            self.cnt[name] = 0
        return name

    def now(self, name):
        self.sem(name)
        return self.cnt[name]

    def op(self, eng, fn, reads=(), writes=(), dma_sem=None):
        waits = []
        for b in reads:
            waits += self.wmap.get(b, [])
        for b in writes:
            waits += self.rmap.get(b, [])
            waits += self.wmap.get(b, [])
        waits += self.fence_pending[eng]
        self.fence_pending[eng] = []
        wl = []
        for sname, v in waits:
            if v > 0 and self.hwm[eng].get(sname, -1) < v:
                self.hwm[eng][sname] = v
                wl.append((sname, v))
        if dma_sem is not None:
            sname, n = dma_sem, 16
        else:
            sname, n = "e_" + eng, 1
        self.sem(sname)
        self.cnt[sname] += n
        mark = (sname, self.cnt[sname])
        for b in writes:
            self.wmap[b] = [mark]
            self.rmap[b] = []
        for b in reads:
            self.rmap.setdefault(b, []).append(mark)
        self.q[eng].append((wl, fn, mark if True else None, n))

    def t(self, fn, **kw): self.op("tensor", fn, **kw)
    def v(self, fn, **kw): self.op("vector", fn, **kw)
    def s(self, fn, **kw): self.op("scalar", fn, **kw)
    def g(self, fn, **kw): self.op("gpsimd", fn, **kw)

    def dma(self, qeng, out, in_, reads=(), writes=()):
        self.op(qeng, lambda e: e.dma_start(out=out, in_=in_),
                reads=reads, writes=writes, dma_sem="d_" + qeng)

    def dma_final_wait(self, qeng):
        sname = self.sem("d_" + qeng)
        v = self.cnt[sname]
        self.q[qeng].append(([(sname, v)], None, None, 0))

    def fence(self):
        marks = []
        for e in ENGS:
            s1 = "e_" + e
            if self.now(s1):
                marks.append((s1, self.cnt[s1]))
            s2 = "d_" + e
            if s2 in self.cnt and self.cnt[s2]:
                marks.append((s2, self.cnt[s2]))
        for e in ENGS:
            self.fence_pending[e] = list(marks)
        self.wmap = {}
        self.rmap = {}

    def flush(self, block):
        for e in ENGS:
            if not self.q[e]:
                continue
            items = self.q[e]
            self.q[e] = []
            def body(h, items=items):
                for wl, fn, mark, n in items:
                    for sname, v in wl:
                        h.wait_ge(self.sems[sname], v)
                    if fn is None:
                        continue
                    r = fn(h)
                    if mark is not None:
                        r.then_inc(self.sems[mark[0]], n)
            getattr(block, e)(body)


def pack_l0_wkv(wk, wv):
    out = np.zeros((8, 128, 2048), np.float32)
    for d in range(8):
        for h in range(16):
            out[d, :, h * 128:h * 128 + 64] = wk[d * 128:(d + 1) * 128, h * 64:(h + 1) * 64]
            out[d, :, h * 128 + 64:h * 128 + 128] = wv[d * 128:(d + 1) * 128, h * 64:(h + 1) * 64]
    return np.ascontiguousarray(out)


def pack_l0_bkv(bk, bv):
    out = np.zeros((128, 16), np.float32)
    for h in range(16):
        out[0:64, h] = bk[h * 64:(h + 1) * 64]
        out[64:128, h] = bv[h * 64:(h + 1) * 64]
    return np.ascontiguousarray(out)


def pack_headcols(vec, H=16, HD=64):
    return np.ascontiguousarray(vec.reshape(H, HD).T)


def pack_b1(b1):
    return np.ascontiguousarray(b1.reshape(8, 32, 128).transpose(0, 2, 1))


def ln_phase(s, cfg, x_hbm, xn_hbm, ntile, epsc, xn_own_hbm=None, own_t0=0, tag="ln"):
    nc = s.nc
    with ExitStack() as st_:
        NB = 3
        xs = [st_.enter_context(nc.sbuf_tensor(f"{tag}_x{i}", [128, 1024], F32)) for i in range(NB)]
        xnb = [st_.enter_context(nc.sbuf_tensor(f"{tag}_n{i}", [128, 1024], BF16)) for i in range(NB)]
        xnf = [st_.enter_context(nc.sbuf_tensor(f"{tag}_f{i}", [128, 1024], F32)) for i in range(NB)]
        stt = st_.enter_context(nc.sbuf_tensor(tag + "_st", [128, 8], F32))
        sq = st_.enter_context(nc.sbuf_tensor(tag + "_sq", [128, 1024], F32))
        for i in range(ntile):
            b = i % NB
            X, N, F = f"{tag}x{b}", f"{tag}n{b}", f"{tag}f{b}"
            s.dma("sync", xs[b][:, :], x_hbm[i * 128:(i + 1) * 128, :], writes=[X])
            s.v(lambda e, b=b: e.reduce_sum(stt[:, 0:1], xs[b][:, :], axis=AX),
                reads=[X], writes=["lnst"])
            s.s(lambda e, b=b: e.activation(sq[:, :], xs[b][:, :], AF.Square),
                reads=[X], writes=["lnsq"])
            s.v(lambda e: e.reduce_sum(stt[:, 1:2], sq[:, :], axis=AX),
                reads=["lnsq"], writes=["lnst"])
            s.v(lambda e: e.tensor_scalar(stt[:, 2:3], stt[:, 0:1], 1.0 / 1024, None, ALU.mult),
                reads=["lnst"], writes=["lnst"])
            s.v(lambda e: e.tensor_scalar(stt[:, 3:4], stt[:, 1:2], 1.0 / 1024, None, ALU.mult),
                reads=["lnst"], writes=["lnst"])
            s.v(lambda e: e.tensor_tensor(stt[:, 4:5], stt[:, 2:3], stt[:, 2:3], ALU.mult),
                reads=["lnst"], writes=["lnst"])
            s.v(lambda e: e.tensor_tensor(stt[:, 5:6], stt[:, 3:4], stt[:, 4:5], ALU.subtract),
                reads=["lnst"], writes=["lnst"])
            s.s(lambda e: e.activation(stt[:, 6:7], stt[:, 5:6], AF.Ln, bias=epsc[:, 0:1]),
                reads=["lnst", "consts"], writes=["lnst"])
            s.s(lambda e: e.activation(stt[:, 7:8], stt[:, 6:7], AF.Exp, scale=-0.5),
                reads=["lnst"], writes=["lnst"])
            s.v(lambda e, b=b: e.tensor_scalar(xnb[b][:, :], xs[b][:, :], stt[:, 2:3], stt[:, 7:8],
                                               ALU.subtract, ALU.mult),
                reads=[X, "lnst"], writes=[N])
            s.dma("gpsimd", xn_hbm[i * 128:(i + 1) * 128, :], xnb[b][:, :],
                  reads=[N], writes=[f"hbm_xn_{i}"])
            if xn_own_hbm is not None and own_t0 <= i < own_t0 + cfg.OT:
                s.v(lambda e, b=b: e.tensor_scalar(xnf[b][:, :], xs[b][:, :], stt[:, 2:3], stt[:, 7:8],
                                                   ALU.subtract, ALU.mult),
                    reads=[X, "lnst"], writes=[F])
                j = i - own_t0
                s.dma("gpsimd", xn_own_hbm[j * 128:(j + 1) * 128, :], xnf[b][:, :],
                      reads=[F], writes=[f"hbm_xnf_{j}"])
    s.fence()


def l0_kv_phase(s, cfg, pst, xn16, K_h, V_h, wkv, bkv, identb):
    nc = s.nc
    H, W, SEQ = cfg.H, cfg.W512, cfg.SEQ
    with ExitStack() as st_:
        wkvb = [st_.enter_context(nc.sbuf_tensor(f"wkvb{d}", [128, 2048], BF16)) for d in range(8)]
        bkvs = st_.enter_context(nc.sbuf_tensor("bkvs", [128, 16], F32))
        for d in range(8):
            s.dma("gpsimd", wkvb[d][:, :], wkv[d, :, :], writes=["wkvb"])
        s.dma("sync", bkvs[:, :], bkv[:, :], writes=["bkvs"])
        xtk = [st_.enter_context(nc.sbuf_tensor(f"kv_xtk{i}", [128, 4096], BF16)) for i in range(2)]
        xw = [st_.enter_context(nc.sbuf_tensor(f"kv_xw{d}", [128, 1024], BF16)) for d in range(8)]
        kvsb = [st_.enter_context(nc.sbuf_tensor(f"kv_sb{i}", [128, 512], BF16)) for i in range(2)]
        vstg = [st_.enter_context(nc.sbuf_tensor(f"kv_vs{i}", [128, 256], BF16)) for i in range(2)]
        psT = [pst.enter_context(nc.psum_tensor(f"kv_psT{i}", [128, 1024], BF16)) for i in range(2)]
        psKV = [pst.enter_context(nc.psum_tensor(f"kv_psKV{i}", [128, 512], F32)) for i in range(2)]
        psT2 = [pst.enter_context(nc.psum_tensor(f"kv_psT2{i}", [128, 1024], BF16)) for i in range(2)]
        for w in range(W):
            bw = w % 2
            XT = f"xtk{bw}"
            for a in range(4):
                s.dma("sync", xtk[bw][:, a * 1024:(a + 1) * 1024],
                      xn16[w * 512 + a * 128:w * 512 + (a + 1) * 128, :],
                      reads=[f"hbm_xn_{w * 4 + a}"], writes=[XT])
            for d in range(8):
                PT = f"kvpsT{bw}"
                for a in range(4):
                    s.t(lambda e, bw=bw, d=d, a=a: e.transpose(
                        psT[bw][:, a * 128:(a + 1) * 128],
                        xtk[bw][:, a * 1024 + d * 128:a * 1024 + (d + 1) * 128],
                        identb[:, :]),
                        reads=[XT, "identb"], writes=[PT])
                s.s(lambda e, bw=bw, d=d: e.activation(xw[d][:, bw * 512:(bw + 1) * 512],
                                                       psT[bw][:, 0:512], AF.Identity),
                    reads=[PT], writes=[f"xw{d}_{bw}"])
            for h in range(H):
                bh = h % 2
                PK = f"kvpsKV{bh}"
                for d in range(8):
                    s.t(lambda e, bh=bh, d=d, h=h, bw=bw: e.matmul(
                        psKV[bh][:, :], wkvb[d][:, h * 128:(h + 1) * 128],
                        xw[d][:, bw * 512:(bw + 1) * 512],
                        start=(d == 0), stop=(d == 7)),
                        reads=[f"xw{d}_{bw}", "wkvb"], writes=[PK])
                KS = f"kvsb{bh}"
                s.v(lambda e, bh=bh, h=h: e.tensor_scalar(
                    kvsb[bh][:, :], psKV[bh][:, :], bkvs[:, h:h + 1], None, ALU.add),
                    reads=[PK, "bkvs"], writes=[KS])
                s.dma("scalar", K_h[h, :, w * 512:(w + 1) * 512], kvsb[bh][0:64, :],
                      reads=[KS], writes=[f"hbm_K{h}_{w}"])
                PT2 = f"kvpsT2{bh}"
                for j in range(4):
                    s.t(lambda e, bh=bh, j=j: e.transpose(
                        psT2[bh][:, j * 64:(j + 1) * 64],
                        kvsb[bh][64:128, j * 128:(j + 1) * 128],
                        identb[64:128, 64:128]),
                        reads=[KS, "identb"], writes=[PT2])
                VS = f"kvvs{bh}"
                s.s(lambda e, bh=bh: e.activation(vstg[bh][:, :], psT2[bh][:, 0:256], AF.Identity),
                    reads=[PT2], writes=[VS])
                s.dma("scalar", V_h[h, :, w * 256:(w + 1) * 256], vstg[bh][:, :],
                      reads=[VS], writes=[f"hbm_V{h}_{w}"])
    s.fence()


def l0_attn_phase(s, cfg, pst, xn16, K_h, V_h, s_h, wqb, wob, bqcs, bo_bc,
                  scale4, identb, onesb64, epsc, dbg_aps=None):
    nc = s.nc
    H, W, SEQ, OWN = cfg.H, cfg.W512, cfg.SEQ, cfg.OWN
    NK = SEQ // 128
    with ExitStack() as st_:
        xnTo = [st_.enter_context(nc.sbuf_tensor(f"at_xnTo{d}", [128, OWN], BF16)) for d in range(8)]
        xld = [st_.enter_context(nc.sbuf_tensor(f"at_xld{i}", [128, 1024], BF16)) for i in range(2)]
        augk = st_.enter_context(nc.sbuf_tensor("at_augk", [66, SEQ], BF16))
        ksqw = st_.enter_context(nc.sbuf_tensor("at_ksqw", [64, 512], BF16))
        knsm = st_.enter_context(nc.sbuf_tensor("at_knsm", [1, 512], BF16))
        qaug = st_.enter_context(nc.sbuf_tensor("at_qaug", [66, OWN], BF16))
        vtok = st_.enter_context(nc.sbuf_tensor("at_vtok", [128, NK * 64], BF16))
        sqb = [st_.enter_context(nc.sbuf_tensor(f"at_sqb{i}", [128, 512], F32)) for i in range(2)]
        rbuf = [st_.enter_context(nc.sbuf_tensor(f"at_rbuf{i}", [128, 512], F32)) for i in range(2)]
        scb = st_.enter_context(nc.sbuf_tensor("at_scb", [128, SEQ], F32))
        pbuf = st_.enter_context(nc.sbuf_tensor("at_pbuf", [128, SEQ], BF16))
        mx = st_.enter_context(nc.sbuf_tensor("at_mx", [128, 24], F32))
        ptb = [st_.enter_context(nc.sbuf_tensor(f"at_ptb{i}", [128, 512], BF16)) for i in range(2)]
        ctx = [st_.enter_context(nc.sbuf_tensor(f"at_ctx{d}", [128, OWN], BF16)) for d in range(8)]
        psA = [pst.enter_context(nc.psum_tensor(f"at_psA{i}", [128, 512], F32)) for i in range(2)]
        psD = [pst.enter_context(nc.psum_tensor(f"at_psD{i}", [128, 512], F32)) for i in range(2)]
        psP = [pst.enter_context(nc.psum_tensor(f"at_psP{i}", [128, 1024], BF16)) for i in range(2)]
        psC = pst.enter_context(nc.psum_tensor("at_psC", [128, 512], F32))
        psR = pst.enter_context(nc.psum_tensor("at_psR", [128, 512], F32))

        for i in range(cfg.OT):
            bi = i % 2
            XL = f"atxld{bi}"
            s.dma("sync", xld[bi][:, :], xn16[i * 128:(i + 1) * 128, :],
                  reads=[f"hbm_xn_{i}"], writes=[XL])
            PT = f"atpsP{bi}"
            for d in range(8):
                s.t(lambda e, bi=bi, d=d: e.transpose(
                    psP[bi][:, d * 128:(d + 1) * 128],
                    xld[bi][:, d * 128:(d + 1) * 128], identb[:, :]),
                    reads=[XL, "identb"], writes=[PT])
            for d in range(8):
                s.s(lambda e, bi=bi, d=d, i=i: e.activation(
                    xnTo[d][:, i * 128:(i + 1) * 128],
                    psP[bi][:, d * 128:(d + 1) * 128], AF.Identity),
                    reads=[PT], writes=[f"xnTo{d}"])

        for h in range(H):
            s.dma("sync", augk[0:64, :], K_h[h, :, :],
                  reads=[f"hbm_K{h}_{w}" for w in range(W)], writes=["augk_k"])
            s.dma("sync", vtok[:, :], V_h[h, :, :],
                  reads=[f"hbm_V{h}_{w}" for w in range(W)], writes=["vtok"])
            s.g(lambda e: e.memset(augk[64:66, :], 1.0), writes=["augk_a"])
            for w in range(W):
                s.s(lambda e, w=w: e.activation(ksqw[:, :], augk[0:64, w * 512:(w + 1) * 512],
                                                AF.Square),
                    reads=["augk_k"], writes=["ksqw"])
                s.t(lambda e, w=w: e.matmul(psR[0:1, 0:512], onesb64[:, 0:1],
                                            ksqw[:, :], start=True, stop=True),
                    reads=["ksqw", "onesb64"], writes=["psR"])
                s.s(lambda e, w=w: e.activation(knsm[0:1, :],
                                                psR[0:1, 0:512], AF.Identity, bias=epsc[0:1, 0:1]),
                    reads=["psR", "consts"], writes=["knsm"])
                s.dma("scalar", augk[65:66, w * 512:(w + 1) * 512], knsm[0:1, :],
                      reads=["knsm"], writes=["augk_a"])
            s.g(lambda e: e.memset(qaug[64:66, :], 1.0), writes=["qaug_a"])
            for qw in range(cfg.OW):
                for d in range(8):
                    s.t(lambda e, qw=qw, d=d, h=h: e.matmul(
                        psC[0:64, :], wqb[d][:, h * 64:(h + 1) * 64],
                        xnTo[d][:, qw * 512:(qw + 1) * 512],
                        start=(d == 0), stop=(d == 7)),
                        reads=[f"xnTo{d}", "wqb"], writes=["psC"])
                s.v(lambda e, qw=qw, h=h: e.tensor_scalar(
                    qaug[0:64, qw * 512:(qw + 1) * 512], psC[0:64, :],
                    bqcs[:, h:h + 1], -2.0, ALU.add, ALU.mult),
                    reads=["psC", "bqcs"], writes=["qaug_q"])
            for qw in range(cfg.OW):
                s.s(lambda e, qw=qw: e.activation(ksqw[:, :], qaug[0:64, qw * 512:(qw + 1) * 512],
                                                  AF.Square),
                    reads=["qaug_q"], writes=["ksqw"])
                s.t(lambda e, qw=qw: e.matmul(psR[0:1, 0:512], onesb64[:, 0:1],
                                              ksqw[:, :], start=True, stop=True),
                    reads=["ksqw", "onesb64"], writes=["psR"])
                s.s(lambda e, qw=qw: e.activation(qaug[64:65, qw * 512:(qw + 1) * 512],
                                                  psR[0:1, 0:512], AF.Identity, scale=0.25),
                    reads=["psR"], writes=["qaug_a"])
            if dbg_aps is not None and h == 0:
                s.dma("scalar", dbg_aps["dq"][:, :], qaug[:, :], reads=["qaug_q", "qaug_a"],
                      writes=["hbm_dq"])
                s.dma("scalar", dbg_aps["dk"][:, :], augk[:, :], reads=["augk_k", "augk_a"],
                      writes=["hbm_dk"])
            for qt in range(cfg.OT):
                for c in range(W):
                    bc = c % 2
                    PA, PD = f"psA{bc}", f"psD{bc}"
                    s.t(lambda e, bc=bc, qt=qt, c=c: e.matmul(
                        psA[bc][:, :], qaug[0:64, qt * 128:(qt + 1) * 128],
                        augk[0:64, c * 512:(c + 1) * 512], start=True, stop=True),
                        reads=["qaug_q", "augk_k"], writes=[PA])
                    s.t(lambda e, bc=bc, qt=qt, c=c: e.matmul(
                        psD[bc][:, :], qaug[0:66, qt * 128:(qt + 1) * 128],
                        augk[0:66, c * 512:(c + 1) * 512], start=True, stop=True),
                        reads=["qaug_q", "qaug_a", "augk_k", "augk_a"], writes=[PD])
                    SQ, RB = f"sqb{bc}", f"rbuf{bc}"
                    s.s(lambda e, bc=bc: e.activation(sqb[bc][:, :], psA[bc][:, :], AF.Square),
                        reads=[PA], writes=[SQ])
                    s.v(lambda e, bc=bc: e.reciprocal(rbuf[bc][:, :], psD[bc][:, :]),
                        reads=[PD], writes=[RB])
                    s.g(lambda e, bc=bc, c=c: e.tensor_tensor(
                        scb[:, c * 512:(c + 1) * 512], sqb[bc][:, :], rbuf[bc][:, :], ALU.mult),
                        reads=[SQ, RB], writes=[f"scb{c}"])
                    s.v(lambda e, c=c: e.reduce_max(mx[:, c:c + 1], scb[:, c * 512:(c + 1) * 512], axis=AX),
                        reads=[f"scb{c}"], writes=["mxc"])
                s.v(lambda e: e.reduce_max(mx[:, 16:17], mx[:, 0:W], axis=AX),
                    reads=["mxc"], writes=["mx16"])
                s.v(lambda e: e.tensor_scalar(mx[:, 17:18], mx[:, 16:17], scale4[:, 0:1], -1.0,
                                              ALU.mult, ALU.mult),
                    reads=["mx16", "scale4"], writes=["mx17"])
                s.s(lambda e: e.activation(pbuf[:, 0:SEQ], scb[:, 0:SEQ], AF.Exp,
                                           bias=mx[:, 17:18], scale=scale4[:, 0:1],
                                           accum_out=mx[:, 18:19]),
                    reads=[f"scb{c}" for c in range(W)] + ["mx17", "scale4"],
                    writes=["pbuf", "mx18"])
                s.v(lambda e: e.reciprocal(mx[:, 19:20], mx[:, 18:19]),
                    reads=["mx18"], writes=["mx19"])
                s.v(lambda e: e.tensor_scalar(pbuf[:, 0:SEQ], pbuf[:, 0:SEQ], mx[:, 19:20], None,
                                              ALU.mult),
                    reads=["pbuf", "mx19"], writes=["pbuf"])
                if dbg_aps is not None and h == 0 and qt == 0:
                    s.dma("scalar", dbg_aps["dsc"][:, :], scb[:, :],
                          reads=[f"scb{c}" for c in range(W)], writes=["hbm_dsc"])
                    s.dma("scalar", dbg_aps["dpb"][:, :], pbuf[:, :], reads=["pbuf"],
                          writes=["hbm_dpb"])
                for gblk in range(NK // 4):
                    bg = gblk % 2
                    PP, PTB = f"atpsP{bg}", f"ptb{bg}"
                    for u in range(4):
                        kt = gblk * 4 + u
                        s.t(lambda e, bg=bg, u=u, kt=kt: e.transpose(
                            psP[bg][:, u * 128:(u + 1) * 128],
                            pbuf[:, kt * 128:(kt + 1) * 128], identb[:, :]),
                            reads=["pbuf", "identb"], writes=[PP])
                    s.s(lambda e, bg=bg: e.activation(ptb[bg][:, :], psP[bg][:, 0:512], AF.Identity),
                        reads=[PP], writes=[PTB])
                    for u in range(4):
                        kt = gblk * 4 + u
                        s.t(lambda e, bg=bg, u=u, kt=kt: e.matmul(
                            psC[0:64, 0:128], vtok[:, kt * 64:(kt + 1) * 64],
                            ptb[bg][:, u * 128:(u + 1) * 128],
                            start=(kt == 0), stop=(kt == NK - 1)),
                            reads=[PTB, "vtok"], writes=["psC"])
                s.s(lambda e, h=h, qt=qt: e.activation(
                    ctx[h // 2][(h % 2) * 64:(h % 2) * 64 + 64, qt * 128:(qt + 1) * 128],
                    psC[0:64, 0:128], AF.Identity),
                    reads=["psC"], writes=[f"ctx{h // 2}"])
        if dbg_aps is not None:
            s.dma("scalar", dbg_aps["dctx"][:, :], ctx[0][:, :],
                  reads=["ctx0"], writes=["hbm_dctx"])
            s.dma("scalar", dbg_aps["dbo"][:, :], bo_bc[:, :],
                  reads=["bo_bc"], writes=["hbm_dbo"])
        for qt in range(cfg.OT):
            for nb in range(2):
                bo_ = nb
                PA = f"psA{bo_}"
                for d in range(8):
                    s.t(lambda e, qt=qt, nb=nb, d=d, bo_=bo_: e.matmul(
                        psA[bo_][:, :], ctx[d][:, qt * 128:(qt + 1) * 128],
                        wob[d][:, nb * 512:(nb + 1) * 512],
                        start=(d == 0), stop=(d == 7)),
                        reads=[f"ctx{d}", "wob"], writes=[PA])
                OS = f"sqb{nb}"
                s.v(lambda e, nb=nb, bo_=bo_: e.tensor_tensor(
                    sqb[nb][:, :], psA[bo_][:, :], bo_bc[:, nb * 512:(nb + 1) * 512], ALU.add),
                    reads=[PA, "bo_bc"], writes=[OS])
                s.dma("scalar", s_h[qt * 128:(qt + 1) * 128, nb * 512:(nb + 1) * 512],
                      sqb[nb][:, :], reads=[OS], writes=[f"hbm_s_{qt}_{nb}"])
    s.fence()


def l0_ffn_phase(s, cfg, pst, xb, xn16, s_h, x1o, w1, b1c, w2, b2r, gatew, identf, identb):
    nc = s.nc
    OWN = cfg.OWN
    with ExitStack() as st_:
        xnTo = [st_.enter_context(nc.sbuf_tensor(f"ff_xnTo{d}", [128, OWN], BF16)) for d in range(8)]
        xld = [st_.enter_context(nc.sbuf_tensor(f"ff_xld{i}", [128, 1024], BF16)) for i in range(2)]
        gwb = st_.enter_context(nc.sbuf_tensor("ff_gwb", [128, 64], BF16))
        comb = st_.enter_context(nc.sbuf_tensor("ff_comb", [128, 128], F32))
        combT = st_.enter_context(nc.sbuf_tensor("ff_combT", [8, OWN], BF16))
        b2b = st_.enter_context(nc.sbuf_tensor("ff_b2b", [8, 1024], BF16))
        cst = st_.enter_context(nc.sbuf_tensor("ff_cst", [128, 8], F32))
        w1b = [st_.enter_context(nc.sbuf_tensor(f"ff_w1b{d}", [128, 2048], BF16)) for d in range(8)]
        w2b = [st_.enter_context(nc.sbuf_tensor(f"ff_w2b{m}", [128, 1024], BF16)) for m in range(16)]
        b1sb = st_.enter_context(nc.sbuf_tensor("ff_b1sb", [128, 32], F32))
        h1b = [st_.enter_context(nc.sbuf_tensor(f"ff_h1b{m}", [128, 512], BF16)) for m in range(16)]
        acc = [st_.enter_context(nc.sbuf_tensor(f"ff_acc{i}", [128, 1024], F32)) for i in range(cfg.OT)]
        sld = [st_.enter_context(nc.sbuf_tensor(f"ff_sld{i}", [128, 1024], F32)) for i in range(1)]
        xrl = [st_.enter_context(nc.sbuf_tensor(f"ff_xrl{i}", [128, 1024], F32)) for i in range(1)]
        gsl = st_.enter_context(nc.sbuf_tensor("ff_gsl", [128, 1024], F32))
        x1s = [st_.enter_context(nc.sbuf_tensor(f"ff_x1s{i}", [128, 1024], F32)) for i in range(1)]
        psG = pst.enter_context(nc.psum_tensor("ff_psG", [128, 512], F32))
        psH = [pst.enter_context(nc.psum_tensor(f"ff_psH{i}", [128, 512], F32)) for i in range(2)]
        psY = [pst.enter_context(nc.psum_tensor(f"ff_psY{i}", [128, 512], F32)) for i in range(2)]
        psX = pst.enter_context(nc.psum_tensor("ff_psX", [128, 512], F32))

        psPb = pst.enter_context(nc.psum_tensor("ff_psPb", [128, 1024], BF16))
        for i in range(cfg.OT):
            bi = i % 2
            XL = f"ffxld{bi}"
            s.dma("sync", xld[bi][:, :], xn16[i * 128:(i + 1) * 128, :],
                  reads=[f"hbm_xn_{i}"], writes=[XL])
            PT = "ffpsPb"
            for d in range(8):
                s.t(lambda e, bi=bi, d=d: e.transpose(
                    psPb[:, d * 128:(d + 1) * 128],
                    xld[bi][:, d * 128:(d + 1) * 128], identb[:, :]),
                    reads=[XL, "identb"], writes=[PT])
            for d in range(8):
                s.s(lambda e, d=d, i=i: e.activation(
                    xnTo[d][:, i * 128:(i + 1) * 128],
                    psPb[:, d * 128:(d + 1) * 128], AF.Identity),
                    reads=[PT], writes=[f"ffxnTo{d}"])

        for d in range(8):
            s.dma("gpsimd", gwb[:, d * 8:(d + 1) * 8], gatew[d * 128:(d + 1) * 128, :], writes=["gwb"])
        s.dma("gpsimd", b2b[:, :], b2r[:, :], writes=["b2b"])
        for i in range(cfg.OT):
            for d in range(8):
                s.t(lambda e, i=i, d=d: e.matmul(
                    psG[:, 0:8], xnTo[d][:, i * 128:(i + 1) * 128], gwb[:, d * 8:(d + 1) * 8],
                    start=(d == 0), stop=(d == 7)),
                    reads=[f"ffxnTo{d}", "gwb"], writes=["psG"])
            s.v(lambda e, i=i: e.tensor_copy(comb[:, i * 8 + 0:i * 8 + 8], psG[:, 0:8]),
                reads=["psG"], writes=["comb", "sgv"])
            s.v(lambda e, i=i: e.reduce_max(cst[:, 0:1], comb[:, i * 8:i * 8 + 8], axis=AX),
                reads=["comb"], writes=["cst"])
            s.v(lambda e, i=i: e.tensor_scalar(gsl[:, 0:8], comb[:, i * 8:i * 8 + 8], cst[:, 0:1], None,
                                               ALU.is_ge), reads=["comb", "cst"], writes=["gsl8"])
            s.v(lambda e, i=i: e.scalar_tensor_tensor(out=gsl[:, 8:16], in0=gsl[:, 0:8], scalar=-1e30,
                                                      in1=comb[:, i * 8:i * 8 + 8],
                                                      op0=ALU.mult, op1=ALU.add),
                reads=["gsl8", "comb"], writes=["gsl16"])
            s.v(lambda e: e.reduce_max(cst[:, 1:2], gsl[:, 8:16], axis=AX),
                reads=["gsl16"], writes=["cst"])
            s.v(lambda e: e.tensor_tensor(cst[:, 2:3], cst[:, 0:1], cst[:, 1:2], ALU.subtract),
                reads=["cst"], writes=["cst"])
            s.s(lambda e: e.activation(cst[:, 3:4], cst[:, 2:3], AF.Sigmoid),
                reads=["cst"], writes=["cst3"])
            s.v(lambda e: e.tensor_scalar(cst[:, 4:5], cst[:, 3:4], -1.0, 1.0, ALU.mult, ALU.add),
                reads=["cst3"], writes=["cst4"])
            s.v(lambda e, i=i: e.tensor_scalar(gsl[:, 16:24], comb[:, i * 8:i * 8 + 8], cst[:, 1:2], None,
                                               ALU.is_ge), reads=["comb", "cst"], writes=["gsl24"])
            s.v(lambda e: e.tensor_tensor(gsl[:, 24:32], gsl[:, 16:24], gsl[:, 0:8], ALU.subtract),
                reads=["gsl24", "gsl8"], writes=["gsl32"])
            s.v(lambda e, i=i: e.tensor_scalar(gsl[:, 32:40], gsl[:, 0:8], cst[:, 3:4], None, ALU.mult),
                reads=["gsl8", "cst3"], writes=["gsl40"])
            s.v(lambda e, i=i: e.scalar_tensor_tensor(out=comb[:, i * 8:i * 8 + 8], in0=gsl[:, 24:32],
                                                      scalar=cst[:, 4:5], in1=gsl[:, 32:40],
                                                      op0=ALU.mult, op1=ALU.add),
                reads=["gsl32", "gsl40", "cst4"], writes=["comb"])
            s.t(lambda e, i=i: e.transpose(psX[0:8, 0:128], comb[:, i * 8:i * 8 + 8], identf[:, :]),
                reads=["comb", "identf"], writes=["psXc"])
            s.s(lambda e, i=i: e.activation(combT[0:8, i * 128:(i + 1) * 128], psX[0:8, 0:128],
                                            AF.Identity),
                reads=["psXc"], writes=["combT"])
        for i in range(cfg.OT):
            for nb in range(2):
                s.t(lambda e, i=i, nb=nb: e.matmul(
                    psY[nb][:, :], combT[0:8, i * 128:(i + 1) * 128],
                    b2b[0:8, nb * 512:(nb + 1) * 512], start=True, stop=True),
                    reads=["combT", "b2b"], writes=[f"psY{nb}"])
                s.s(lambda e, i=i, nb=nb: e.activation(acc[i][:, nb * 512:(nb + 1) * 512],
                                                       psY[nb][:, :], AF.Identity),
                    reads=[f"psY{nb}"], writes=[f"acc{i}"])
        for ei in range(8):
            s.dma("sync", b1sb[:, :], b1c[ei, :, :], writes=["b1sb"])
            for half in range(2):
                for d in range(8):
                    s.dma("gpsimd", w1b[d][:, :], w1[ei, d * 128:(d + 1) * 128,
                                                     half * 2048:(half + 1) * 2048],
                          writes=[f"w1b{d}"])
                for m in range(16):
                    s.dma("gpsimd", w2b[m][:, :],
                          w2[ei, half * 2048 + m * 128:half * 2048 + (m + 1) * 128, :],
                          writes=[f"w2b{m}"])
                for tw in range(cfg.OW):
                    for m in range(16):
                        bm = m % 2
                        PH = f"psH{bm}"
                        for d in range(8):
                            s.t(lambda e, m=m, d=d, tw=tw, bm=bm: e.matmul(
                                psH[bm][:, :], w1b[d][:, m * 128:(m + 1) * 128],
                                xnTo[d][:, tw * 512:(tw + 1) * 512],
                                start=(d == 0), stop=(d == 7)),
                                reads=[f"w1b{d}", f"ffxnTo{d}"], writes=[PH])
                        s.s(lambda e, m=m, bm=bm, half=half: e.activation(
                            h1b[m][:, :], psH[bm][:, :], AF.Gelu,
                            bias=b1sb[:, half * 16 + m:half * 16 + m + 1]),
                            reads=[PH, "b1sb"], writes=[f"h1b{m}"])
                    for tt_ in range(4):
                        i = tw * 4 + tt_
                        for nb in range(2):
                            by = nb
                            PY = f"psY{by}"
                            for m in range(16):
                                s.t(lambda e, m=m, tt_=tt_, nb=nb, by=by: e.matmul(
                                    psY[by][:, :], h1b[m][:, tt_ * 128:(tt_ + 1) * 128],
                                    w2b[m][:, nb * 512:(nb + 1) * 512],
                                    start=(m == 0), stop=(m == 15)),
                                    reads=[f"h1b{m}", f"w2b{m}"], writes=[PY])
                            s.v(lambda e, i=i, nb=nb, by=by, ei=ei: e.scalar_tensor_tensor(
                                out=acc[i][:, nb * 512:(nb + 1) * 512], in0=psY[by][:, :],
                                scalar=comb[:, i * 8 + ei:i * 8 + ei + 1],
                                in1=acc[i][:, nb * 512:(nb + 1) * 512],
                                op0=ALU.mult, op1=ALU.add),
                                reads=[PY, "comb", f"acc{i}"], writes=[f"acc{i}"])
        for i in range(cfg.OT):
            bi = 0
            s.dma("sync", sld[bi][:, :], s_h[i * 128:(i + 1) * 128, :],
                  reads=[f"hbm_s_{i}_0", f"hbm_s_{i}_1"], writes=[f"sld{bi}"])
            s.dma("sync", xrl[bi][:, :], xb[i * 128:(i + 1) * 128, :], writes=[f"xrl{bi}"])
            s.s(lambda e, i=i: e.activation(gsl[:, :], acc[i][:, :], AF.Silu),
                reads=[f"acc{i}"],
                writes=["gslt", "gsl8", "gsl16", "gsl24", "gsl32", "gsl40"])
            s.v(lambda e, bi=bi: e.tensor_tensor(gsl[:, :], gsl[:, :], sld[bi][:, :], ALU.mult),
                reads=["gslt", f"sld{bi}"], writes=["gslt"])
            s.v(lambda e, bi=bi: e.tensor_tensor(x1s[bi][:, :], xrl[bi][:, :], gsl[:, :], ALU.add),
                reads=["gslt", f"xrl{bi}"], writes=[f"x1s{bi}"])
            s.dma("scalar", x1o[i * 128:(i + 1) * 128, :], x1s[bi][:, :],
                  reads=[f"x1s{bi}"], writes=[f"hbm_x1_{i}"])
    s.fence()


def build_launch_A(cfg=CFG, dbg=False):
    nc = bass.Bass("TRN2", target_bir_lowering=False)
    SEQ, DM, H, OWN = cfg.SEQ, cfg.DM, cfg.H, cfg.OWN
    KD = dict(kind="ExternalOutput") if dbg else {}

    xb = nc.dram_tensor("xb", [SEQ, DM], F32, kind="ExternalInput").ap()
    wkv = nc.dram_tensor("wkv", [8, 128, 2048], F32, kind="ExternalInput").ap()
    wq = nc.dram_tensor("wq", [DM, DM], F32, kind="ExternalInput").ap()
    wo = nc.dram_tensor("wo", [DM, DM], F32, kind="ExternalInput").ap()
    bkv = nc.dram_tensor("bkv", [128, 16], F32, kind="ExternalInput").ap()
    bqc = nc.dram_tensor("bqc", [64, 16], F32, kind="ExternalInput").ap()
    bo = nc.dram_tensor("bo", [1, DM], F32, kind="ExternalInput").ap()
    alpha = nc.dram_tensor("alpha", [1, 1], F32, kind="ExternalInput").ap()
    gatew = nc.dram_tensor("gatew", [DM, 8], F32, kind="ExternalInput").ap()
    w1 = nc.dram_tensor("w1", [8, DM, 4096], F32, kind="ExternalInput").ap()
    b1c = nc.dram_tensor("b1c", [8, 128, 32], F32, kind="ExternalInput").ap()
    w2 = nc.dram_tensor("w2", [8, 4096, DM], F32, kind="ExternalInput").ap()
    b2r = nc.dram_tensor("b2r", [8, DM], F32, kind="ExternalInput").ap()
    x1o = nc.dram_tensor("x1o", [OWN, DM], F32, kind="ExternalOutput").ap()

    xn16 = nc.dram_tensor("xn16", [SEQ, DM], BF16, **KD).ap()
    K_h = nc.dram_tensor("K_h", [H, 64, SEQ], BF16, **KD).ap()
    V_h = nc.dram_tensor("V_h", [H, 128, (SEQ // 128) * 64], BF16, **KD).ap()
    s_h = nc.dram_tensor("s_h", [OWN, DM], F32, **KD).ap()

    with ExitStack() as stack:
        s = Sched(nc, stack)
        onesf = stack.enter_context(nc.sbuf_tensor("onesf", [1, 128], F32))
        onesb64 = stack.enter_context(nc.sbuf_tensor("onesb64", [64, 1], BF16))
        identb = stack.enter_context(nc.sbuf_tensor("identb", [128, 128], BF16))
        identf = stack.enter_context(nc.sbuf_tensor("identf", [128, 128], F32))
        scale4 = stack.enter_context(nc.sbuf_tensor("scale4", [128, 1], F32))
        al_sb = stack.enter_context(nc.sbuf_tensor("al_sb", [1, 1], F32))
        consts = stack.enter_context(nc.sbuf_tensor("consts", [128, 4], F32))
        s.g(lambda e: e.memset(consts[:, 0:1], 1e-5), writes=["consts"])
        s.g(lambda e: e.memset(consts[:, 1:2], math.log(0.25)), writes=["consts"])
        s.g(lambda e: e.memset(onesf[:, :], 1.0), writes=["onesf"])
        s.g(lambda e: e.memset(onesb64[:, :], 1.0), writes=["onesb64"])
        for t_, key in ((identb, "identb"), (identf, "identf")):
            s.g(lambda e, t_=t_: e.memset(t_[:, :], 1.0), writes=[key])
            s.g(lambda e, t_=t_: e.affine_select(out=t_[:, :], in_=t_[:, :],
                                                 compare_op=ALU.is_equal, fill=0.0,
                                                 base=0, pattern=[[-1, 128]], channel_multiplier=1),
                reads=[key], writes=[key])
        s.dma("sync", al_sb[0:1, 0:1], alpha[0:1, 0:1], writes=["al"])
        lns0 = math.log(8.0 / math.log(65.0))
        s.s(lambda e: e.activation(al_sb[0:1, 0:1], al_sb[0:1, 0:1], AF.Exp,
                                   scale=lns0, bias=consts[0:1, 1:2]),
            reads=["al", "consts"], writes=["al"])

        with ExitStack() as wstack:
            wqb = [wstack.enter_context(nc.sbuf_tensor(f"wqb{d}", [128, 1024], BF16)) for d in range(8)]
            wob = [wstack.enter_context(nc.sbuf_tensor(f"wob{d}", [128, 1024], BF16)) for d in range(8)]
            bqcs = wstack.enter_context(nc.sbuf_tensor("bqcs", [64, 16], F32))
            bo_bc = wstack.enter_context(nc.sbuf_tensor("bo_bc", [128, DM], F32))
            for d in range(8):
                s.dma("gpsimd", wqb[d][:, :], wq[d * 128:(d + 1) * 128, :], writes=["wqb"])
                s.dma("gpsimd", wob[d][:, :], wo[d * 128:(d + 1) * 128, :], writes=["wob"])
            s.dma("sync", bqcs[:, :], bqc[:, :], writes=["bqcs"])

            with ExitStack() as pst0:
                bo_sb = pst0.enter_context(nc.sbuf_tensor("bo_sb", [1, DM], F32))
                psB = pst0.enter_context(nc.psum_tensor("psB", [128, 512], F32))
                s.dma("sync", bo_sb[0:1, :], bo[0:1, :], writes=["bo_sb"])
                s.t(lambda e: e.matmul(psB[:, 0:1], onesf[0:1, :], al_sb[0:1, 0:1],
                                       start=True, stop=True),
                    reads=["al", "onesf"], writes=["psB"])
                s.s(lambda e: e.activation(scale4[:, 0:1], psB[:, 0:1], AF.Identity),
                    reads=["psB"], writes=["scale4"])
                for half in range(2):
                    s.t(lambda e, half=half: e.matmul(psB[:, 0:512], onesf[0:1, :],
                                                      bo_sb[0:1, half * 512:(half + 1) * 512],
                                                      start=True, stop=True),
                        reads=["bo_sb", "onesf"], writes=["psB"])
                    s.s(lambda e, half=half: e.activation(bo_bc[:, half * 512:(half + 1) * 512],
                                                          psB[:, 0:512], AF.Identity),
                        reads=["psB"], writes=["bo_bc"])
            s.fence()
            s.wmap["bo_bc"] = []
            s.wmap["scale4"] = []

            ln_phase(s, cfg, xb, xn16, cfg.TT, consts, tag="ln0")
            with ExitStack() as pst1:
                l0_kv_phase(s, cfg, pst1, xn16, K_h, V_h, wkv, bkv, identb)
            with ExitStack() as pst2:
                dbg_aps = None
                if dbg:
                    dbg_aps = dict(
                        dq=nc.dram_tensor("dq", [66, OWN], BF16, kind="ExternalOutput").ap(),
                        dk=nc.dram_tensor("dk", [66, SEQ], BF16, kind="ExternalOutput").ap(),
                        dsc=nc.dram_tensor("dsc", [128, SEQ], F32, kind="ExternalOutput").ap(),
                        dpb=nc.dram_tensor("dpb", [128, SEQ], BF16, kind="ExternalOutput").ap(),
                        dctx=nc.dram_tensor("dctx", [128, OWN], BF16, kind="ExternalOutput").ap(),
                        dbo=nc.dram_tensor("dbo", [128, DM], F32, kind="ExternalOutput").ap(),
                    )
                l0_attn_phase(s, cfg, pst2, xn16, K_h, V_h, s_h, wqb, wob, bqcs, bo_bc,
                              scale4, identb, onesb64, consts, dbg_aps)
        with ExitStack() as pst3:
            l0_ffn_phase(s, cfg, pst3, xb, xn16, s_h, x1o, w1, b1c, w2, b2r, gatew, identf, identb)

        for qe in ("sync", "gpsimd", "scalar"):
            s.dma_final_wait(qe)
        with nc.Block() as block:
            s.flush(block)
    return nc


if __name__ == "__main__":
    nc = build_launch_A(Cfg(SEQ=2048))
    print("built")


def prep_A(inputs, cfg=CFG):
    x = np.asarray(inputs["L0_attn_x"], np.float32)
    common = dict(
        wkv=pack_l0_wkv(np.asarray(inputs["L0_attn_wk"], np.float32),
                        np.asarray(inputs["L0_attn_wv"], np.float32)),
        wq=np.ascontiguousarray(np.asarray(inputs["L0_attn_wq"], np.float32)),
        wo=np.ascontiguousarray(np.asarray(inputs["L0_attn_wo"], np.float32)),
        bkv=pack_l0_bkv(np.asarray(inputs["L0_attn_bk"], np.float32),
                        np.asarray(inputs["L0_attn_bv"], np.float32)),
        bqc=pack_headcols(np.asarray(inputs["L0_attn_bq"], np.float32)),
        bo=np.ascontiguousarray(np.asarray(inputs["L0_attn_bo"], np.float32).reshape(1, -1)),
        alpha=np.asarray(inputs["L0_attn_alpha"], np.float32).reshape(1, 1),
        gatew=np.ascontiguousarray(np.asarray(inputs["L0_ffn_gate_w"], np.float32)),
        w1=np.ascontiguousarray(np.asarray(inputs["L0_ffn_w1"], np.float32)),
        b1c=pack_b1(np.asarray(inputs["L0_ffn_b1"], np.float32)),
        w2=np.ascontiguousarray(np.asarray(inputs["L0_ffn_w2"], np.float32)),
        b2r=np.ascontiguousarray(np.asarray(inputs["L0_ffn_b2"], np.float32)),
    )
    in_maps = []
    for c in range(cfg.NCORE if hasattr(cfg, "NCORE") else 8):
        b, j = c // cfg.GROUP, c % cfg.GROUP
        xroll = np.ascontiguousarray(np.roll(x[b], -j * cfg.OWN, axis=0))
        in_maps.append(dict(common, xb=xroll))

    def assemble(results):
        x1 = np.zeros_like(x)
        for c in range(8):
            b, j = c // cfg.GROUP, c % cfg.GROUP
            x1[b, j * cfg.OWN:(j + 1) * cfg.OWN] = results[c]["x1o"]
        return x1
    return in_maps, assemble


def build_launch_B1(cfg=CFG):
    nc = bass.Bass("TRN2", target_bir_lowering=False)
    SEQ, DM, OWN = cfg.SEQ, cfg.DM, cfg.OWN
    x1b = nc.dram_tensor("x1b", [SEQ, DM], F32, kind="ExternalInput").ap()
    xn2T = nc.dram_tensor("xn2T", [DM, SEQ], BF16, kind="ExternalOutput").ap()
    xn2f = nc.dram_tensor("xn2f", [OWN, DM], F32, kind="ExternalOutput").ap()
    own_t0 = nc.dram_tensor("own_t0_unused", [1, 1], F32, kind="ExternalInput").ap()
    xn16 = nc.dram_tensor("xn16b", [SEQ, DM], BF16).ap()

    with ExitStack() as stack:
        s = Sched(nc, stack)
        consts = stack.enter_context(nc.sbuf_tensor("consts", [128, 4], F32))
        identb = stack.enter_context(nc.sbuf_tensor("identb", [128, 128], BF16))
        s.g(lambda e: e.memset(consts[:, 0:1], 1e-5), writes=["consts"])
        s.g(lambda e: e.memset(identb[:, :], 1.0), writes=["identb"])
        s.g(lambda e: e.affine_select(out=identb[:, :], in_=identb[:, :],
                                      compare_op=ALU.is_equal, fill=0.0,
                                      base=0, pattern=[[-1, 128]], channel_multiplier=1),
            reads=["identb"], writes=["identb"])
        ln_phase(s, cfg, x1b, xn16, cfg.TT, consts, xn_own_hbm=xn2f, own_t0=0, tag="lnb")
        with ExitStack() as pst:
            psP = [pst.enter_context(nc.psum_tensor(f"b1psP{i}", [128, 1024], BF16)) for i in range(2)]
            with ExitStack() as st_:
                xld = [st_.enter_context(nc.sbuf_tensor(f"b1x{i}", [128, 1024], BF16)) for i in range(2)]
                xstg = [st_.enter_context(nc.sbuf_tensor(f"b1s{i}", [128, 1024], BF16)) for i in range(2)]
                for i in range(cfg.TT):
                    bi = i % 2
                    XL, PT, SG = f"bx{bi}", f"bp{bi}", f"bs{bi}"
                    s.dma("sync", xld[bi][:, :], xn16[i * 128:(i + 1) * 128, :],
                          reads=[f"hbm_xn_{i}"], writes=[XL])
                    for d in range(8):
                        s.t(lambda e, bi=bi, d=d: e.transpose(
                            psP[bi][:, d * 128:(d + 1) * 128],
                            xld[bi][:, d * 128:(d + 1) * 128], identb[:, :]),
                            reads=[XL, "identb"], writes=[PT])
                    s.s(lambda e, bi=bi: e.activation(xstg[bi][:, :], psP[bi][:, 0:1024], AF.Identity),
                        reads=[PT], writes=[SG])
                    for d in range(8):
                        s.dma("gpsimd", xn2T[d * 128:(d + 1) * 128, i * 128:(i + 1) * 128],
                              xstg[bi][:, d * 128:(d + 1) * 128],
                              reads=[SG], writes=[f"hbm_T_{i}_{d}"])
        for qe in ("sync", "gpsimd", "scalar"):
            s.dma_final_wait(qe)
        with nc.Block() as block:
            s.flush(block)
    return nc


def build_launch_B2(cfg=CFG):
    nc = bass.Bass("TRN2", target_bir_lowering=False)
    SEQ, DM, OWN, H = cfg.SEQ, cfg.DM, cfg.OWN, cfg.H
    W2 = SEQ // 512
    xn2T = nc.dram_tensor("xn2T", [DM, SEQ], BF16, kind="ExternalInput").ap()
    xqT = nc.dram_tensor("xqT", [DM, OWN], BF16, kind="ExternalInput").ap()
    mask1 = nc.dram_tensor("mask1", [128, 512], BF16, kind="ExternalInput").ap()
    wdq = nc.dram_tensor("wdq", [64, 512], F32, kind="ExternalInput").ap()
    wuq = nc.dram_tensor("wuq", [512, 64], F32, kind="ExternalInput").ap()
    wdkv = nc.dram_tensor("wdkv", [64, 1024], F32, kind="ExternalInput").ap()
    wukv = nc.dram_tensor("wukv", [1024, 128], F32, kind="ExternalInput").ap()
    wkr = nc.dram_tensor("wkr", [64, 64], F32, kind="ExternalInput").ap()
    wqr = nc.dram_tensor("wqr", [64, 64], F32, kind="ExternalInput").ap()
    wot = nc.dram_tensor("wot", [64, 64], F32, kind="ExternalInput").ap()
    qnw = nc.dram_tensor("qnw", [64, 1], F32, kind="ExternalInput").ap()
    knw = nc.dram_tensor("knw", [64, 1], F32, kind="ExternalInput").ap()
    s2o = nc.dram_tensor("s2o", [cfg.OT, 128, DM], BF16, kind="ExternalOutput").ap()

    RMSEPS = 1.1920929e-07
    with ExitStack() as stack:
        s = Sched(nc, stack)
        consts = stack.enter_context(nc.sbuf_tensor("consts", [128, 4], F32))
        identb = stack.enter_context(nc.sbuf_tensor("identb", [128, 128], BF16))
        onesb64 = stack.enter_context(nc.sbuf_tensor("onesb64", [64, 1], BF16))
        onesb1 = stack.enter_context(nc.sbuf_tensor("onesb1", [1, 64], BF16))
        s.g(lambda e: e.memset(consts[:, 0:1], RMSEPS), writes=["consts"])
        s.g(lambda e: e.memset(consts[:, 1:2], math.log(0.125)), writes=["consts"])
        s.g(lambda e: e.memset(onesb64[:, :], 1.0), writes=["onesb64"])
        s.g(lambda e: e.memset(onesb1[:, :], 1.0), writes=["onesb1"])
        s.g(lambda e: e.memset(identb[:, :], 1.0), writes=["identb"])
        s.g(lambda e: e.affine_select(out=identb[:, :], in_=identb[:, :],
                                      compare_op=ALU.is_equal, fill=0.0,
                                      base=0, pattern=[[-1, 128]], channel_multiplier=1),
            reads=["identb"], writes=["identb"])
        wdqb = stack.enter_context(nc.sbuf_tensor("wdqb", [64, 512], BF16))
        wuqb = stack.enter_context(nc.sbuf_tensor("wuqb", [128, 4 * 64], BF16))
        wdkvb = stack.enter_context(nc.sbuf_tensor("wdkvb", [64, 1024], BF16))
        wukvb = stack.enter_context(nc.sbuf_tensor("wukvb", [128, 8 * 128], BF16))
        wkrb = stack.enter_context(nc.sbuf_tensor("wkrb", [64, 64], BF16))
        wqrb = stack.enter_context(nc.sbuf_tensor("wqrb", [64, 64], BF16))
        wotb = stack.enter_context(nc.sbuf_tensor("wotb", [64, 64], BF16))
        qnws = stack.enter_context(nc.sbuf_tensor("qnws", [64, 1], F32))
        knws = stack.enter_context(nc.sbuf_tensor("knws", [64, 1], F32))
        mask_s = stack.enter_context(nc.sbuf_tensor("mask_s", [128, 512], BF16))
        s.dma("gpsimd", wdqb[:, :], wdq[:, :], writes=["wdqb"])
        for lc in range(4):
            s.dma("gpsimd", wuqb[:, lc * 64:(lc + 1) * 64], wuq[lc * 128:(lc + 1) * 128, :],
                  writes=["wuqb"])
        s.dma("gpsimd", wdkvb[:, :], wdkv[:, :], writes=["wdkvb"])
        for lc in range(8):
            s.dma("gpsimd", wukvb[:, lc * 128:(lc + 1) * 128], wukv[lc * 128:(lc + 1) * 128, :],
                  writes=["wukvb"])
        s.dma("gpsimd", wkrb[:, :], wkr[:, :], writes=["wkrb"])
        s.dma("gpsimd", wqrb[:, :], wqr[:, :], writes=["wqrb"])
        s.dma("gpsimd", wotb[:, :], wot[:, :], writes=["wotb"])
        s.dma("sync", qnws[:, :], qnw[:, :], writes=["qnws"])
        s.dma("sync", knws[:, :], knw[:, :], writes=["knws"])
        s.dma("sync", mask_s[:, :], mask1[:, :], writes=["mask_s"])

        with ExitStack() as pst, ExitStack() as st_:
            xh = st_.enter_context(nc.sbuf_tensor("xh", [64, SEQ], BF16))
            xqh = st_.enter_context(nc.sbuf_tensor("xqh", [64, OWN], BF16))
            cq = st_.enter_context(nc.sbuf_tensor("cq", [128, 4 * 512], BF16))
            qcn = st_.enter_context(nc.sbuf_tensor("qcn", [64, 512], BF16))
            qfull = st_.enter_context(nc.sbuf_tensor("qfull", [128, OWN], BF16))
            ckv = st_.enter_context(nc.sbuf_tensor("ckv", [128, 8 * 512], BF16))
            kfull = st_.enter_context(nc.sbuf_tensor("kfull", [128, SEQ], BF16))
            vtok = st_.enter_context(nc.sbuf_tensor("vtok", [128, (SEQ // 128) * 64], BF16))
            vfm = st_.enter_context(nc.sbuf_tensor("vfm", [64, 512], BF16))
            sqw = st_.enter_context(nc.sbuf_tensor("sqw", [64, 512], BF16))
            rrow = st_.enter_context(nc.sbuf_tensor("rrow", [1, 512], BF16))
            kcsb = st_.enter_context(nc.sbuf_tensor("kcsb", [64, 512], BF16))
            scb = st_.enter_context(nc.sbuf_tensor("scb", [128, SEQ], F32))
            pbuf = st_.enter_context(nc.sbuf_tensor("pbuf", [128, SEQ], BF16))
            mx = st_.enter_context(nc.sbuf_tensor("mx", [128, 24], F32))
            ptb = [st_.enter_context(nc.sbuf_tensor(f"ptb{i}", [128, 512], BF16)) for i in range(2)]
            ctmp = st_.enter_context(nc.sbuf_tensor("ctmp", [64, 128], BF16))
            s2b = [st_.enter_context(nc.sbuf_tensor(f"s2b{i}", [128, DM], BF16)) for i in range(cfg.OT)]
            psK = [pst.enter_context(nc.psum_tensor(f"b2psK{i}", [128, 512], F32)) for i in range(2)]
            psM = pst.enter_context(nc.psum_tensor("b2psM", [128, 512], F32))
            psR = pst.enter_context(nc.psum_tensor("b2psR", [128, 512], F32))
            psA = [pst.enter_context(nc.psum_tensor(f"b2psA{i}", [128, 512], F32)) for i in range(2)]
            psP = pst.enter_context(nc.psum_tensor("b2psP", [128, 1024], BF16))
            psC = pst.enter_context(nc.psum_tensor("b2psC", [128, 512], F32))

            for h in range(H):
                s.dma("sync", xh[:, :], xn2T[h * 64:(h + 1) * 64, :], writes=["xh"])
                s.dma("sync", xqh[:, :], xqT[h * 64:(h + 1) * 64, :], writes=["xqh"])
                for qw in range(cfg.OW):
                    for lc in range(4):
                        s.t(lambda e, lc=lc, qw=qw: e.matmul(
                            psK[lc % 2][:, :], wdqb[:, lc * 128:(lc + 1) * 128],
                            xqh[:, qw * 512:(qw + 1) * 512], start=True, stop=True),
                            reads=["wdqb", "xqh"], writes=[f"b2psK{lc % 2}"])
                        s.s(lambda e, lc=lc: e.activation(cq[:, lc * 512:(lc + 1) * 512],
                                                          psK[lc % 2][:, :], AF.Identity),
                            reads=[f"b2psK{lc % 2}"], writes=["cq"])
                    for lc in range(4):
                        s.t(lambda e, lc=lc: e.matmul(
                            psM[0:64, :], wuqb[:, lc * 64:(lc + 1) * 64],
                            cq[:, lc * 512:(lc + 1) * 512],
                            start=(lc == 0), stop=(lc == 3)),
                            reads=["wuqb", "cq"], writes=["psM"])
                    s.s(lambda e: e.activation(sqw[:, :], psM[0:64, :], AF.Square),
                        reads=["psM"], writes=["sqw"])
                    s.t(lambda e: e.matmul(psR[0:1, :], onesb64[:, 0:1], sqw[:, :],
                                           start=True, stop=True),
                        reads=["sqw", "onesb64"], writes=["psR"])
                    s.s(lambda e: e.activation(rrow[0:1, :], psR[0:1, :], AF.Ln,
                                               scale=1.0 / 64, bias=consts[0:1, 0:1]),
                        reads=["psR", "consts"], writes=["rrow"])
                    s.s(lambda e: e.activation(rrow[0:1, :], rrow[0:1, :], AF.Exp, scale=-0.5),
                        reads=["rrow"], writes=["rrow"])
                    s.t(lambda e: e.matmul(psR[0:64, :], onesb1[0:1, :], rrow[0:1, :],
                                           start=True, stop=True),
                        reads=["rrow", "onesb1"], writes=["psR"])
                    s.s(lambda e: e.activation(kcsb[:, :], psM[0:64, :], AF.Identity),
                        reads=["psM"], writes=["kcsb"])
                    s.v(lambda e: e.scalar_tensor_tensor(out=qcn[:, :], in0=kcsb[:, :],
                                                         scalar=qnws[:, 0:1], in1=psR[0:64, :],
                                                         op0=ALU.mult, op1=ALU.mult),
                        reads=["kcsb", "qnws", "psR"], writes=["qcn"])
                    s.v(lambda e, qw=qw: e.tensor_copy(qfull[0:64, qw * 512:(qw + 1) * 512], qcn[:, :]),
                        reads=["qcn"], writes=["qfull"])
                    s.t(lambda e, qw=qw: e.matmul(psM[0:64, :], wqrb[:, :], qcn[:, :],
                                                  start=True, stop=True),
                        reads=["wqrb", "qcn"], writes=["psM"])
                    s.s(lambda e, qw=qw: e.activation(qfull[64:128, qw * 512:(qw + 1) * 512],
                                                      psM[0:64, :], AF.Identity),
                        reads=["psM"], writes=["qfull"])
                for sw in range(W2):
                    for lc in range(8):
                        s.t(lambda e, lc=lc, sw=sw: e.matmul(
                            psK[lc % 2][:, :], wdkvb[:, lc * 128:(lc + 1) * 128],
                            xh[:, sw * 512:(sw + 1) * 512], start=True, stop=True),
                            reads=["wdkvb", "xh"], writes=[f"b2psK{lc % 2}"])
                        s.s(lambda e, lc=lc: e.activation(ckv[:, lc * 512:(lc + 1) * 512],
                                                          psK[lc % 2][:, :], AF.Identity),
                            reads=[f"b2psK{lc % 2}"], writes=["ckv"])
                    for lc in range(8):
                        s.t(lambda e, lc=lc: e.matmul(
                            psM[:, :], wukvb[:, lc * 128:(lc + 1) * 128],
                            ckv[:, lc * 512:(lc + 1) * 512],
                            start=(lc == 0), stop=(lc == 7)),
                            reads=["wukvb", "ckv"], writes=["psM"])
                    s.s(lambda e: e.activation(sqw[:, :], psM[0:64, :], AF.Square),
                        reads=["psM"], writes=["sqw"])
                    s.t(lambda e: e.matmul(psR[0:1, :], onesb64[:, 0:1], sqw[:, :],
                                           start=True, stop=True),
                        reads=["sqw", "onesb64"], writes=["psR"])
                    s.s(lambda e: e.activation(rrow[0:1, :], psR[0:1, :], AF.Ln,
                                               scale=1.0 / 64, bias=consts[0:1, 0:1]),
                        reads=["psR", "consts"], writes=["rrow"])
                    s.s(lambda e: e.activation(rrow[0:1, :], rrow[0:1, :], AF.Exp, scale=-0.5,
                                               bias=consts[0:1, 1:2]),
                        reads=["rrow", "consts"], writes=["rrow"])
                    s.t(lambda e: e.matmul(psR[0:64, :], onesb1[0:1, :], rrow[0:1, :],
                                           start=True, stop=True),
                        reads=["rrow", "onesb1"], writes=["psR"])
                    s.s(lambda e: e.activation(kcsb[:, :], psM[0:64, :], AF.Identity),
                        reads=["psM"], writes=["kcsb"])
                    s.v(lambda e, sw=sw: e.scalar_tensor_tensor(
                        out=kfull[0:64, sw * 512:(sw + 1) * 512], in0=kcsb[:, :],
                        scalar=knws[:, 0:1], in1=psR[0:64, :], op0=ALU.mult, op1=ALU.mult),
                        reads=["kcsb", "knws", "psR"], writes=["kfull"])
                    s.s(lambda e: e.activation(vfm[:, :], psM[64:128, :], AF.Identity),
                        reads=["psM"], writes=["vfm"])
                    for j in range(4):
                        s.t(lambda e, j=j: e.transpose(
                            psP[:, j * 64:(j + 1) * 64], vfm[:, j * 128:(j + 1) * 128],
                            identb[0:64, 0:64]),
                            reads=["vfm", "identb"], writes=["psPb2"])
                    s.s(lambda e, sw=sw: e.activation(vtok[:, sw * 256:(sw + 1) * 256],
                                                      psP[:, 0:256], AF.Identity),
                        reads=["psPb2"], writes=["vtok"])
                    s.t(lambda e, sw=sw: e.matmul(psM[0:64, :], wkrb[:, :],
                                                  xh[:, sw * 512:(sw + 1) * 512],
                                                  start=True, stop=True),
                        reads=["wkrb", "xh"], writes=["psM"])
                    s.s(lambda e, sw=sw: e.activation(kfull[64:128, sw * 512:(sw + 1) * 512],
                                                      psM[0:64, :], AF.Identity, scale=0.125),
                        reads=["psM"], writes=["kfull"])
                for k in range(cfg.OT):
                    for c in range(k + 1):
                        bc = c % 2
                        PA = f"b2psA{bc}"
                        s.t(lambda e, bc=bc, k=k, c=c: e.matmul(
                            psA[bc][:, :], qfull[:, k * 128:(k + 1) * 128],
                            kfull[:, c * 512:(c + 1) * 512], start=True, stop=True),
                            reads=["qfull", "kfull"], writes=[PA])
                        if c < k:
                            s.v(lambda e, bc=bc, c=c, k=k: e.tensor_scalar(
                                scb[:, c * 512:(c + 1) * 512], psA[bc][:, :], 0.0,
                                (-1e30 if c == 0 else mx[:, 20:21]), ALU.add, ALU.max,
                                accum_out=mx[:, 20:21]),
                                reads=[PA] + ([] if c == 0 else ["mx20"]),
                                writes=[f"scb{c}", "mx20"])
                        else:
                            s.v(lambda e, bc=bc, c=c: e.tensor_tensor(
                                scb[:, c * 512:(c + 1) * 512], psA[bc][:, :], mask_s[:, :],
                                ALU.add),
                                reads=[PA, "mask_s"], writes=[f"scb{c}"])
                            s.v(lambda e, c=c, k=k: e.tensor_scalar(
                                scb[:, c * 512:(c + 1) * 512], scb[:, c * 512:(c + 1) * 512],
                                0.0, (-1e30 if c == 0 else mx[:, 20:21]), ALU.add, ALU.max,
                                accum_out=mx[:, 20:21]),
                                reads=[f"scb{c}"] + ([] if c == 0 else ["mx20"]),
                                writes=[f"scb{c}", "mx20"])
                    KL = (k + 1) * 512
                    s.v(lambda e: e.tensor_scalar(mx[:, 17:18], mx[:, 20:21], -1.0, None, ALU.mult),
                        reads=["mx20"], writes=["mx17"])
                    s.s(lambda e, KL=KL: e.activation(pbuf[:, 0:KL], scb[:, 0:KL], AF.Exp,
                                                      bias=mx[:, 17:18],
                                                      accum_out=mx[:, 18:19]),
                        reads=[f"scb{c}" for c in range(k + 1)] + ["mx17"],
                        writes=["pbuf", "mx18"])
                    s.v(lambda e: e.reciprocal(mx[:, 19:20], mx[:, 18:19]),
                        reads=["mx18"], writes=["mx19"])
                    nk4 = k + 1
                    for g4 in range(nk4):
                        bg = g4 % 2
                        PTB = f"ptb{bg}"
                        for u in range(4):
                            kt = g4 * 4 + u
                            s.t(lambda e, u=u, kt=kt: e.transpose(
                                psP[:, 512 + u * 128:512 + (u + 1) * 128] if False else psP[:, u * 128:(u + 1) * 128],
                                pbuf[:, kt * 128:(kt + 1) * 128], identb[:, :]),
                                reads=["pbuf", "identb"], writes=["psPb2"])
                        s.s(lambda e, bg=bg: e.activation(ptb[bg][:, :], psP[:, 0:512], AF.Identity),
                            reads=["psPb2"], writes=[PTB])
                        for u in range(4):
                            kt = g4 * 4 + u
                            s.t(lambda e, bg=bg, u=u, kt=kt: e.matmul(
                                psC[0:64, 0:128], vtok[:, kt * 64:(kt + 1) * 64],
                                ptb[bg][:, u * 128:(u + 1) * 128],
                                start=(kt == 0), stop=(kt == nk4 * 4 - 1)),
                                reads=[PTB, "vtok"], writes=["psC"])
                    s.s(lambda e: e.activation(ctmp[:, :], psC[0:64, 0:128], AF.Identity),
                        reads=["psC"], writes=["ctmp"])
                    s.t(lambda e: e.matmul(psM[:, 0:64], ctmp[:, :], wotb[:, :],
                                           start=True, stop=True),
                        reads=["ctmp", "wotb"], writes=["psM"])
                    s.v(lambda e, k=k, h=h: e.tensor_scalar(
                        s2b[k][:, h * 64:(h + 1) * 64], psM[:, 0:64], mx[:, 19:20], None,
                        ALU.mult),
                        reads=["psM", "mx19"], writes=[f"s2b{k}"])
            for k in range(cfg.OT):
                s.dma("scalar", s2o[k, :, :], s2b[k][:, :], reads=[f"s2b{k}"],
                      writes=[f"hbm_s2_{k}"])
        for qe in ("sync", "gpsimd", "scalar"):
            s.dma_final_wait(qe)
        with nc.Block() as block:
            s.flush(block)
    return nc


def prep_B(inputs, x1, cfg=CFG):
    g32 = lambda k: np.asarray(inputs[k], np.float32)
    b1_maps = []
    for c in range(8):
        b, j = c // cfg.GROUP, c % cfg.GROUP
        xr = np.ascontiguousarray(np.roll(x1[b], -j * cfg.OWN, axis=0))
        b1_maps.append(dict(x1b=xr, own_t0_unused=np.zeros((1, 1), np.float32)))

    def b2_maps(b1_results):
        import ml_dtypes
        common = dict(
            wdq=np.ascontiguousarray(g32("L1_attn_W_DQ").T),
            wuq=np.ascontiguousarray(g32("L1_attn_W_UQ").T),
            wdkv=np.ascontiguousarray(g32("L1_attn_W_DKV").T),
            wukv=np.ascontiguousarray(g32("L1_attn_W_UKV").T),
            wkr=np.ascontiguousarray(g32("L1_attn_W_KR").T),
            wqr=np.ascontiguousarray(g32("L1_attn_W_QR").T),
            wot=np.ascontiguousarray(g32("L1_attn_W_O").T),
            qnw=np.ascontiguousarray(g32("L1_attn_q_norm_w").reshape(64, 1)),
            knw=np.ascontiguousarray(g32("L1_attn_k_norm_w").reshape(64, 1)),
        )
        xn2T_b = {}
        for b in range(2):
            r = b1_results[b * cfg.GROUP]
            xn2T_b[b] = np.asarray(r["xn2T"])
        maps = []
        for c in range(8):
            b, j = c // cfg.GROUP, c % cfg.GROUP
            xt = xn2T_b[b]
            cols = np.concatenate([np.arange((4 * k + j) * 128, (4 * k + j + 1) * 128)
                                   for k in range(cfg.OT)])
            xq = np.ascontiguousarray(xt[:, cols])
            mask = np.zeros((128, 512), np.float32)
            xi = np.arange(512)[None, :]
            pi = np.arange(128)[:, None]
            mask[xi > 128 * j + pi] = NEG
            maps.append(dict(common, xn2T=np.ascontiguousarray(xt),
                             xqT=xq,
                             mask1=mask.astype(ml_dtypes.bfloat16)))
        return maps

    def assemble_s2(b2_results):
        s2 = np.zeros((2, cfg.SEQ, cfg.DM), np.float32)
        for c in range(8):
            b, j = c // cfg.GROUP, c % cfg.GROUP
            r = np.asarray(b2_results[c]["s2o"]).astype(np.float32)
            for k in range(cfg.OT):
                gt = 4 * k + j
                s2[b, gt * 128:(gt + 1) * 128] = r[k]
        return s2

    def xn2f_assemble(b1_results):
        xn2 = np.zeros((2, cfg.SEQ, cfg.DM), np.float32)
        for c in range(8):
            b, j = c // cfg.GROUP, c % cfg.GROUP
            xn2[b, j * cfg.OWN:(j + 1) * cfg.OWN] = b1_results[c]["xn2f"]
        return xn2

    return b1_maps, b2_maps, assemble_s2, xn2f_assemble


CA_HALO = 640


def build_launch_Ca(cfg=CFG, dbg=False):
    nc = bass.Bass("TRN2", target_bir_lowering=False)
    OWN, DM, H = cfg.OWN, cfg.DM, cfg.H
    NZ = CA_HALO + OWN
    NTZ = NZ // 128
    NW = OWN // 256
    zin = nc.dram_tensor("zin", [NZ, DM], F32, kind="ExternalInput").ap()
    wqkvp = nc.dram_tensor("wqkvp", [H, 8, 128, 192], F32, kind="ExternalInput").ap()
    woi = nc.dram_tensor("woi", [DM, DM], F32, kind="ExternalInput").ap()
    qsc = nc.dram_tensor("qsc", [64, 1], F32, kind="ExternalInput").ap()
    ksc = nc.dram_tensor("ksc", [64, 1], F32, kind="ExternalInput").ap()
    csq = nc.dram_tensor("csq", [64, NZ], BF16, kind="ExternalInput").ap()
    ssq = nc.dram_tensor("ssq", [64, NZ], BF16, kind="ExternalInput").ap()
    csk = nc.dram_tensor("csk", [64, NZ], BF16, kind="ExternalInput").ap()
    ssk = nc.dram_tensor("ssk", [64, NZ], BF16, kind="ExternalInput").ap()
    masks = nc.dram_tensor("masks", [NW, 2, 128, 512], BF16, kind="ExternalInput").ap()
    z1o = nc.dram_tensor("z1o", [OWN, DM], F32, kind="ExternalOutput").ap()
    if dbg:
        dqf = nc.dram_tensor("dqf", [64, NZ], BF16, kind="ExternalOutput").ap()
        dkf = nc.dram_tensor("dkf", [64, NZ], BF16, kind="ExternalOutput").ap()
        dvt = nc.dram_tensor("dvt", [128, (NZ // 128) * 64], BF16, kind="ExternalOutput").ap()
        dqk = nc.dram_tensor("dqk", [128, NZ], BF16, kind="ExternalOutput").ap()
        dzT = nc.dram_tensor("dzT", [8, 128, NZ], BF16, kind="ExternalOutput").ap()

    with ExitStack() as stack:
        s = Sched(nc, stack)
        identb = stack.enter_context(nc.sbuf_tensor("identb", [128, 128], BF16))
        onesb64 = stack.enter_context(nc.sbuf_tensor("onesb64", [64, 1], BF16))
        onesb1 = stack.enter_context(nc.sbuf_tensor("onesb1", [1, 64], BF16))
        s.g(lambda e: e.memset(onesb64[:, :], 1.0), writes=["onesb64"])
        s.g(lambda e: e.memset(onesb1[:, :], 1.0), writes=["onesb1"])
        s.g(lambda e: e.memset(identb[:, :], 1.0), writes=["identb"])
        s.g(lambda e: e.affine_select(out=identb[:, :], in_=identb[:, :],
                                      compare_op=ALU.is_equal, fill=0.0,
                                      base=0, pattern=[[-1, 128]], channel_multiplier=1),
            reads=["identb"], writes=["identb"])
        qksc = stack.enter_context(nc.sbuf_tensor("qksc", [128, 1], F32))
        s.dma("sync", qksc[0:64, :], qsc[:, :], writes=["qksc"])
        s.dma("sync", qksc[64:128, :], ksc[:, :], writes=["qksc"])
        tbl = {}
        for nm, src in (("csq", csq), ("ssq", ssq), ("csk", csk), ("ssk", ssk)):
            t_ = stack.enter_context(nc.sbuf_tensor("t" + nm, [64, NZ], BF16))
            s.dma("sync", t_[:, :], src[:, :], writes=[nm])
            tbl[nm] = t_
        maskt = stack.enter_context(nc.sbuf_tensor("maskt", [128, NW * 2 * 512], BF16))
        for w in range(NW):
            for qt in range(2):
                s.dma("sync", maskt[:, (w * 2 + qt) * 512:(w * 2 + qt + 1) * 512],
                      masks[w, qt, :, :], writes=["maskt"])
        wob = [stack.enter_context(nc.sbuf_tensor(f"wob{d}", [128, DM], BF16)) for d in range(8)]
        for d in range(8):
            s.dma("gpsimd", wob[d][:, :], woi[d * 128:(d + 1) * 128, :], writes=["wob"])

        with ExitStack() as pst, ExitStack() as st_:
            zT = [st_.enter_context(nc.sbuf_tensor(f"zT{d}", [128, NZ], BF16)) for d in range(8)]
            zld = [st_.enter_context(nc.sbuf_tensor(f"zld{i}", [128, 1024], BF16)) for i in range(2)]
            zldf = [st_.enter_context(nc.sbuf_tensor(f"zldf{i}", [128, 1024], F32)) for i in range(2)]
            wqh = [st_.enter_context(nc.sbuf_tensor(f"wqh{d}", [128, 192], BF16)) for d in range(8)]
            qkh = st_.enter_context(nc.sbuf_tensor("qkh", [128, NZ], BF16))
            qfin = st_.enter_context(nc.sbuf_tensor("qfin", [64, NZ], BF16))
            kfin = st_.enter_context(nc.sbuf_tensor("kfin", [64, NZ], BF16))
            vtok = st_.enter_context(nc.sbuf_tensor("vtok", [128, (NZ // 128) * 64], BF16))
            vfm = st_.enter_context(nc.sbuf_tensor("vfm", [64, 512], BF16))
            sqw = st_.enter_context(nc.sbuf_tensor("sqw", [64, 512], BF16))
            rrow = st_.enter_context(nc.sbuf_tensor("rrow", [1, 512], BF16))
            qh_t = st_.enter_context(nc.sbuf_tensor("qh_t", [64, 512], BF16))
            qrot = st_.enter_context(nc.sbuf_tensor("qrot", [64, 512], BF16))
            t1b = st_.enter_context(nc.sbuf_tensor("t1b", [64, 512], BF16))
            scb = st_.enter_context(nc.sbuf_tensor("scb", [128, 512], F32))
            nrowt = st_.enter_context(nc.sbuf_tensor("nrowt", [1, 512], F32))
            pbuf = st_.enter_context(nc.sbuf_tensor("pbuf", [128, 512], BF16))
            mx = st_.enter_context(nc.sbuf_tensor("mx", [128, 8], F32))
            ptb = st_.enter_context(nc.sbuf_tensor("ptb", [128, 512], BF16))
            ctx = [st_.enter_context(nc.sbuf_tensor(f"ctx{d}", [128, OWN], BF16)) for d in range(8)]
            z1s = st_.enter_context(nc.sbuf_tensor("z1s", [128, 1024], F32))
            psK = [pst.enter_context(nc.psum_tensor(f"capsK{i}", [128, 512], F32)) for i in range(2)]
            psM = pst.enter_context(nc.psum_tensor("capsM", [128, 512], F32))
            psR = pst.enter_context(nc.psum_tensor("capsR", [128, 512], F32))
            psA = pst.enter_context(nc.psum_tensor("capsA", [128, 512], F32))
            psP = pst.enter_context(nc.psum_tensor("capsP", [128, 1024], BF16))
            psC = pst.enter_context(nc.psum_tensor("capsC", [128, 512], F32))

            for i in range(NTZ):
                bi = i % 2
                ZL = f"zld{bi}"
                s.dma("sync", zldf[bi][:, :], zin[i * 128:(i + 1) * 128, :], writes=[ZL + "f"])
                s.v(lambda e, bi=bi: e.tensor_copy(zld[bi][:, :], zldf[bi][:, :]),
                    reads=[ZL + "f"], writes=[ZL])
                PT = "capsP"
                for d in range(8):
                    s.t(lambda e, bi=bi, d=d: e.transpose(
                        psP[:, d * 128:(d + 1) * 128],
                        zld[bi][:, d * 128:(d + 1) * 128], identb[:, :]),
                        reads=[ZL, "identb"], writes=[PT])
                for d in range(8):
                    s.s(lambda e, d=d, i=i: e.activation(
                        zT[d][:, i * 128:(i + 1) * 128],
                        psP[:, d * 128:(d + 1) * 128], AF.Identity),
                        reads=[PT], writes=[f"zT{d}"])

            KV0 = 128
            NKV = NZ - KV0
            for h in range(H):
                for d in range(8):
                    s.dma("gpsimd", wqh[d][:, :], wqkvp[h, d, :, :], writes=[f"wqh{d}"])
                for w5 in range(NKV // 512):
                    c0 = KV0 + w5 * 512
                    for d in range(8):
                        off = c0 if d < 4 else c0 - 1
                        s.t(lambda e, d=d, off=off: e.matmul(
                            psK[0][:, :], wqh[d][:, 0:128], zT[d][:, off:off + 512],
                            start=(d == 0), stop=(d == 7)),
                            reads=[f"wqh{d}", f"zT{d}"], writes=["capsK0"])
                    s.s(lambda e, c0=c0: e.activation(qkh[:, c0:c0 + 512], psK[0][:, :], AF.Identity),
                        reads=["capsK0"], writes=["qkh"])
                    for d in range(8):
                        off = c0 if d < 4 else c0 - 1
                        s.t(lambda e, d=d, off=off: e.matmul(
                            psK[1][0:64, :], wqh[d][:, 128:192], zT[d][:, off:off + 512],
                            start=(d == 0), stop=(d == 7)),
                            reads=[f"wqh{d}", f"zT{d}"], writes=["capsK1"])
                    s.s(lambda e: e.activation(vfm[:, :], psK[1][0:64, :], AF.Identity),
                        reads=["capsK1"], writes=["vfm"])
                    for j in range(4):
                        s.t(lambda e, j=j: e.transpose(
                            psP[:, j * 64:(j + 1) * 64], vfm[:, j * 128:(j + 1) * 128],
                            identb[0:64, 0:64]),
                            reads=["vfm", "identb"], writes=["capsP"])
                    s.s(lambda e, c0=c0: e.activation(vtok[:, (c0 // 128) * 64:(c0 // 128 + 4) * 64],
                                                      psP[:, 0:256], AF.Identity),
                        reads=["capsP"], writes=["vtok"])
                    for (rows, cs, sn, FIN) in ((slice(0, 64), "csq", "ssq", qfin),
                                                (slice(64, 128), "csk", "ssk", kfin)):
                        RK = "qfin" if rows.start == 0 else "kfin"
                        rb = rows.start
                        s.s(lambda e, rows=rows, c0=c0: e.activation(
                            sqw[:, :], qkh[rows, c0:c0 + 512], AF.Square),
                            reads=["qkh"], writes=["sqw"])
                        s.t(lambda e: e.matmul(psR[0:1, :], onesb64[:, 0:1], sqw[:, :],
                                               start=True, stop=True),
                            reads=["sqw", "onesb64"], writes=["capsR"])
                        s.v(lambda e: e.tensor_scalar(nrowt[0:1, :], psR[0:1, :], 1e-24, None,
                                                      ALU.max),
                            reads=["capsR"], writes=["nrow"])
                        s.s(lambda e: e.activation(rrow[0:1, :], nrowt[0:1, :], AF.Ln),
                            reads=["nrow"], writes=["rrow"])
                        s.s(lambda e: e.activation(rrow[0:1, :], rrow[0:1, :], AF.Exp, scale=-0.5),
                            reads=["rrow"], writes=["rrow"])
                        s.t(lambda e, rb=rb: e.matmul(psR[rb:rb + 64, :], onesb1[0:1, :],
                                                      rrow[0:1, :], start=True, stop=True),
                            reads=["rrow", "onesb1"], writes=["capsR"])
                        s.v(lambda e, rows=rows, c0=c0, rb=rb: e.scalar_tensor_tensor(
                            out=qh_t[:, :], in0=qkh[rows, c0:c0 + 512], scalar=qksc[rows, 0:1],
                            in1=psR[rows, :], op0=ALU.mult, op1=ALU.mult),
                            reads=["qkh", "qksc", "capsR"], writes=["qh_t"])
                        s.g(lambda e: e.tensor_scalar(qrot[0:32, :], qh_t[32:64, :], -1.0, None,
                                                      ALU.mult),
                            reads=["qh_t"], writes=["qrot"])
                        s.g(lambda e: e.tensor_copy(qrot[32:64, :], qh_t[0:32, :]),
                            reads=["qh_t"], writes=["qrot"])
                        s.v(lambda e, cs=cs, c0=c0: e.tensor_tensor(
                            t1b[:, :], qh_t[:, :], tbl[cs][:, c0:c0 + 512], ALU.mult),
                            reads=["qh_t", cs], writes=["t1b"])
                        s.v(lambda e, sn=sn, c0=c0, FIN=FIN, RK=RK: e.tensor_tensor(
                            qrot[:, :], qrot[:, :], tbl[sn][:, c0:c0 + 512], ALU.mult),
                            reads=["qrot", sn], writes=["qrot"])
                        s.v(lambda e, c0=c0, FIN=FIN, RK=RK: e.tensor_tensor(
                            FIN[:, c0:c0 + 512], t1b[:, :], qrot[:, :], ALU.add),
                            reads=["t1b", "qrot"], writes=[RK])
                if dbg and h == 0:
                    for d in range(8):
                        s.dma("scalar", dzT[d, :, :], zT[d][:, :], reads=[f"zT{d}"], writes=["dz_"])
                    s.dma("scalar", dqf[:, :], qfin[:, :], reads=["qfin"], writes=["dq_"])
                    s.dma("scalar", dkf[:, :], kfin[:, :], reads=["kfin"], writes=["dk_"])
                    s.dma("scalar", dvt[:, :], vtok[:, :], reads=["vtok"], writes=["dv_"])
                    s.dma("scalar", dqk[:, :], qkh[:, :], reads=["qkh"], writes=["dqk_"])
                for w in range(NW):
                    qcol0 = CA_HALO + 256 * w
                    kcol0 = qcol0 - 256
                    for qt in range(2):
                        qc = qcol0 + qt * 128
                        s.t(lambda e, qc=qc, kcol0=kcol0: e.matmul(
                            psA[:, :], qfin[:, qc:qc + 128], kfin[:, kcol0:kcol0 + 512],
                            start=True, stop=True),
                            reads=["qfin", "kfin"], writes=["capsA"])
                        mcol = (w * 2 + qt) * 512
                        s.v(lambda e, mcol=mcol: e.tensor_tensor(
                            scb[:, :], psA[:, :], maskt[:, mcol:mcol + 512], ALU.add),
                            reads=["capsA", "maskt"], writes=["scbw"])
                        s.v(lambda e: e.reduce_max(mx[:, 0:1], scb[:, :], axis=AX),
                            reads=["scbw"], writes=["mx0"])
                        s.v(lambda e: e.tensor_scalar(mx[:, 1:2], mx[:, 0:1], -1.0, None, ALU.mult),
                            reads=["mx0"], writes=["mx1"])
                        s.s(lambda e: e.activation(pbuf[:, :], scb[:, :], AF.Exp,
                                                   bias=mx[:, 1:2], accum_out=mx[:, 2:3]),
                            reads=["scbw", "mx1"], writes=["pbufw", "mx2"])
                        s.v(lambda e: e.reciprocal(mx[:, 3:4], mx[:, 2:3]),
                            reads=["mx2"], writes=["mx3"])
                        s.v(lambda e: e.tensor_scalar(pbuf[:, :], pbuf[:, :], mx[:, 3:4], None,
                                                      ALU.mult),
                            reads=["pbufw", "mx3"], writes=["pbufw"])
                        for u in range(4):
                            s.t(lambda e, u=u: e.transpose(
                                psP[:, u * 128:(u + 1) * 128], pbuf[:, u * 128:(u + 1) * 128],
                                identb[:, :]),
                                reads=["pbufw", "identb"], writes=["capsP"])
                        s.s(lambda e: e.activation(ptb[:, :], psP[:, 0:512], AF.Identity),
                            reads=["capsP"], writes=["ptb"])
                        for u in range(4):
                            kt = kcol0 // 128 + u
                            s.t(lambda e, u=u, kt=kt: e.matmul(
                                psC[0:64, 0:128], vtok[:, kt * 64:(kt + 1) * 64],
                                ptb[:, u * 128:(u + 1) * 128],
                                start=(u == 0), stop=(u == 3)),
                                reads=["ptb", "vtok"], writes=["capsC"])
                        occ = w * 256 + qt * 128
                        s.s(lambda e, h=h, occ=occ: e.activation(
                            ctx[h // 2][(h % 2) * 64:(h % 2) * 64 + 64, occ:occ + 128],
                            psC[0:64, 0:128], AF.Identity),
                            reads=["capsC"], writes=[f"ctx{h // 2}"])
            for i in range(cfg.OT):
                bi = i % 2
                s.dma("sync", zldf[bi][:, :], zin[(5 + i) * 128:(6 + i) * 128, :],
                      writes=[f"zld{bi}f"])
                for nb in range(2):
                    bn = nb
                    PA = f"capsK{bn}"
                    for d in range(8):
                        s.t(lambda e, i=i, nb=nb, d=d, bn=bn: e.matmul(
                            psK[bn][:, :], ctx[d][:, i * 128:(i + 1) * 128],
                            wob[d][:, nb * 512:(nb + 1) * 512],
                            start=(d == 0), stop=(d == 7)),
                            reads=[f"ctx{d}", "wob"], writes=[PA])
                    s.v(lambda e, bi=bi, nb=nb, bn=bn: e.tensor_tensor(
                        z1s[:, nb * 512:(nb + 1) * 512], psK[bn][:, :],
                        zldf[bi][:, nb * 512:(nb + 1) * 512], ALU.add),
                        reads=[PA, f"zld{bi}f"], writes=["z1s"])
                s.dma("scalar", z1o[i * 128:(i + 1) * 128, :], z1s[:, :],
                      reads=["z1s"], writes=[f"hbm_z1_{i}"])
        for qe in ("sync", "gpsimd", "scalar"):
            s.dma_final_wait(qe)
        with nc.Block() as block:
            s.flush(block)
    return nc


DIN = 2730
MCH = [128] * 21 + [42]


def build_launch_Cb(cfg=CFG, final=False):
    nc = bass.Bass("TRN2", target_bir_lowering=False)
    OWN, DM = cfg.OWN, cfg.DM
    NZ = 128 + OWN
    NTZ = NZ // 128
    z1in = nc.dram_tensor("z1in", [NZ, DM], F32, kind="ExternalInput").ap()
    w1i = nc.dram_tensor("w1i", [DM, 2 * DIN], F32, kind="ExternalInput").ap()
    w2i = nc.dram_tensor("w2i", [DIN, DM], F32, kind="ExternalInput").ap()
    b1ci = nc.dram_tensor("b1ci", [128, 44], F32, kind="ExternalInput").ap()
    b2i = nc.dram_tensor("b2i", [1, DM], F32, kind="ExternalInput").ap()
    lngc = nc.dram_tensor("lngc", [128, 8], F32, kind="ExternalInput").ap()
    lnbc = nc.dram_tensor("lnbc", [128, 8], F32, kind="ExternalInput").ap()
    if final:
        x1own = nc.dram_tensor("x1own", [OWN, DM], F32, kind="ExternalInput").ap()
        s2own = nc.dram_tensor("s2own", [OWN, DM], BF16, kind="ExternalInput").ap()
    z2o = nc.dram_tensor("z2o", [OWN, DM], F32, kind="ExternalOutput").ap()

    with ExitStack() as stack:
        s = Sched(nc, stack)
        identb = stack.enter_context(nc.sbuf_tensor("identb", [128, 128], BF16))
        onesf = stack.enter_context(nc.sbuf_tensor("onesf", [1, 128], F32))
        oneA = stack.enter_context(nc.sbuf_tensor("oneA", [128, 2], BF16))
        oneB = stack.enter_context(nc.sbuf_tensor("oneB", [128, 2], BF16))
        consts = stack.enter_context(nc.sbuf_tensor("consts", [128, 2], F32))
        s.g(lambda e: e.memset(onesf[:, :], 1.0), writes=["onesf"])
        s.g(lambda e: e.memset(consts[:, 0:1], 1e-5), writes=["consts"])
        s.g(lambda e: e.memset(oneA[:, :], 0.0), writes=["oneA"])
        s.g(lambda e: e.memset(oneA[:, 0:1], 1.0), writes=["oneA"])
        s.g(lambda e: e.memset(oneB[:, :], 0.0), writes=["oneB"])
        s.g(lambda e: e.memset(oneB[:, 1:2], 1.0), writes=["oneB"])
        s.g(lambda e: e.memset(identb[:, :], 1.0), writes=["identb"])
        s.g(lambda e: e.affine_select(out=identb[:, :], in_=identb[:, :],
                                      compare_op=ALU.is_equal, fill=0.0,
                                      base=0, pattern=[[-1, 128]], channel_multiplier=1),
            reads=["identb"], writes=["identb"])
        b1s = stack.enter_context(nc.sbuf_tensor("b1s", [128, 44], F32))
        lngs = stack.enter_context(nc.sbuf_tensor("lngs", [128, 8], F32))
        lnbs = stack.enter_context(nc.sbuf_tensor("lnbs", [128, 8], F32))
        b2s = stack.enter_context(nc.sbuf_tensor("b2s", [1, DM], F32))
        b2bc = stack.enter_context(nc.sbuf_tensor("b2bc", [128, DM], F32))
        s.dma("sync", b1s[:, :], b1ci[:, :], writes=["b1s"])
        s.dma("sync", lngs[:, :], lngc[:, :], writes=["lngs"])
        s.dma("sync", lnbs[:, :], lnbc[:, :], writes=["lnbs"])
        s.dma("sync", b2s[0:1, :], b2i[0:1, :], writes=["b2s"])

        with ExitStack() as pst, ExitStack() as st_:
            zT = [st_.enter_context(nc.sbuf_tensor(f"zT{d}", [128, NZ], BF16)) for d in range(8)]
            zld = [st_.enter_context(nc.sbuf_tensor(f"zld{i}", [128, 1024], BF16)) for i in range(1)]
            sqd = st_.enter_context(nc.sbuf_tensor("sqd", [128, 512], BF16))
            vtmp = st_.enter_context(nc.sbuf_tensor("vtmp", [1, 512], F32))
            srow = st_.enter_context(nc.sbuf_tensor("srow", [2, NZ], F32))
            qrow = st_.enter_context(nc.sbuf_tensor("qrow", [2, NZ], F32))
            srow1 = st_.enter_context(nc.sbuf_tensor("srow1", [1, NZ], BF16))
            qrow1 = st_.enter_context(nc.sbuf_tensor("qrow1", [1, NZ], BF16))
            rows = st_.enter_context(nc.sbuf_tensor("rows", [1, 2 * OWN], F32))
            mubc = st_.enter_context(nc.sbuf_tensor("mubc", [128, OWN], F32))
            rsbc = st_.enter_context(nc.sbuf_tensor("rsbc", [128, OWN], F32))
            xnT = [st_.enter_context(nc.sbuf_tensor(f"xnT{d}", [128, OWN // 4], BF16)) for d in range(8)]
            t1f = st_.enter_context(nc.sbuf_tensor("t1f", [128, 512], F32))
            w1sa = [st_.enter_context(nc.sbuf_tensor(f"w1sa{i}", [128, 8 * 128], BF16)) for i in range(2)]
            w1sg = [st_.enter_context(nc.sbuf_tensor(f"w1sg{i}", [128, 8 * 128], BF16)) for i in range(2)]
            ga = st_.enter_context(nc.sbuf_tensor("ga", [128, OWN // 4], BF16))
            gg = st_.enter_context(nc.sbuf_tensor("gg", [128, OWN // 4], BF16))
            actT = [st_.enter_context(nc.sbuf_tensor(f"actT{m}", [MCH[m], OWN // 4], BF16))
                    for m in range(22)]
            w2b = [st_.enter_context(nc.sbuf_tensor(f"w2b{m}", [MCH[m], 1024], BF16))
                   for m in range(22)]
            zres = st_.enter_context(nc.sbuf_tensor("zres", [128, 1024], F32))
            z2s = st_.enter_context(nc.sbuf_tensor("z2s", [128, 1024], F32))
            if final:
                x1l = st_.enter_context(nc.sbuf_tensor("x1l", [128, 1024], F32))
                s2l = st_.enter_context(nc.sbuf_tensor("s2l", [128, 1024], BF16))
            psP = pst.enter_context(nc.psum_tensor("cbpsP", [128, 1024], BF16))
            psS = pst.enter_context(nc.psum_tensor("cbpsS", [128, 512], F32))
            psQ = pst.enter_context(nc.psum_tensor("cbpsQ", [128, 512], F32))
            psB = pst.enter_context(nc.psum_tensor("cbpsB", [128, 512], F32))
            psH = [pst.enter_context(nc.psum_tensor(f"cbpsH{i}", [128, 512], F32)) for i in range(2)]
            psY = [pst.enter_context(nc.psum_tensor(f"cbpsY{i}", [128, 512], F32)) for i in range(2)]

            for i in range(NTZ):
                bi = 0
                ZL = f"zld{bi}"
                s.dma("gpsimd", zld[bi][:, :], z1in[i * 128:(i + 1) * 128, :], writes=[ZL])
                for d in range(8):
                    s.t(lambda e, bi=bi, d=d: e.transpose(
                        psP[:, d * 128:(d + 1) * 128],
                        zld[bi][:, d * 128:(d + 1) * 128], identb[:, :]),
                        reads=[ZL, "identb"], writes=["cbpsP"])
                for d in range(8):
                    s.s(lambda e, d=d, i=i: e.activation(
                        zT[d][:, i * 128:(i + 1) * 128],
                        psP[:, d * 128:(d + 1) * 128], AF.Identity),
                        reads=["cbpsP"], writes=[f"zT{d}"])
            wins = [(w * 512, 512) for w in range(NZ // 512)] + ([(NZ // 512 * 512, NZ % 512)] if NZ % 512 else [])
            for (c0, wl) in wins:
                for d in range(8):
                    lhs = oneA if d < 4 else oneB
                    LK = "oneA" if d < 4 else "oneB"
                    s.t(lambda e, d=d, c0=c0, wl=wl, lhs=lhs: e.matmul(
                        psS[0:2, 0:wl], lhs[:, :], zT[d][:, c0:c0 + wl],
                        start=(d == 0), stop=(d == 7)),
                        reads=[f"zT{d}", LK], writes=["cbpsS"])
                for d in range(8):
                    lhs = oneA if d < 4 else oneB
                    LK = "oneA" if d < 4 else "oneB"
                    s.s(lambda e, d=d, c0=c0, wl=wl: e.activation(
                        sqd[:, 0:wl], zT[d][:, c0:c0 + wl], AF.Square),
                        reads=[f"zT{d}"], writes=["sqd"])
                    s.t(lambda e, d=d, c0=c0, wl=wl, lhs=lhs: e.matmul(
                        psQ[0:2, 0:wl], lhs[:, :], sqd[:, 0:wl],
                        start=(d == 0), stop=(d == 7)),
                        reads=["sqd", LK], writes=["cbpsQ"])
                s.s(lambda e, c0=c0, wl=wl: e.activation(srow[0:2, c0:c0 + wl], psS[0:2, 0:wl],
                                                         AF.Identity),
                    reads=["cbpsS"], writes=["srow"])
                s.s(lambda e, c0=c0, wl=wl: e.activation(qrow[0:2, c0:c0 + wl], psQ[0:2, 0:wl],
                                                         AF.Identity),
                    reads=["cbpsQ"], writes=["qrow"])
            s.dma("gpsimd", srow1[0:1, :], srow[1:2, :], reads=["srow"], writes=["srow1"])
            s.dma("gpsimd", qrow1[0:1, :], qrow[1:2, :], reads=["qrow"], writes=["qrow1"])
            s.v(lambda e: e.tensor_tensor(rows[0:1, 0:OWN], srow[0:1, 128:NZ],
                                          srow1[0:1, 127:NZ - 1], ALU.add),
                reads=["srow", "srow1"], writes=["rows"])
            s.v(lambda e: e.tensor_scalar(rows[0:1, 0:OWN], rows[0:1, 0:OWN], 1.0 / 1024, None,
                                          ALU.mult), reads=["rows"], writes=["rows"])
            s.v(lambda e: e.tensor_tensor(rows[0:1, OWN:2 * OWN], qrow[0:1, 128:NZ],
                                          qrow1[0:1, 127:NZ - 1], ALU.add),
                reads=["qrow", "qrow1", "rows"], writes=["rows"])
            s.v(lambda e: e.tensor_scalar(rows[0:1, OWN:2 * OWN], rows[0:1, OWN:2 * OWN],
                                          1.0 / 1024, None, ALU.mult), reads=["rows"], writes=["rows"])
            for w in range(OWN // 512):
                s.v(lambda e, w=w: e.tensor_tensor(vtmp[0:1, :], rows[0:1, w * 512:(w + 1) * 512],
                                                   rows[0:1, w * 512:(w + 1) * 512], ALU.mult),
                    reads=["rows"], writes=["vtmp"])
                s.v(lambda e, w=w: e.tensor_tensor(rows[0:1, OWN + w * 512:OWN + (w + 1) * 512],
                                                   rows[0:1, OWN + w * 512:OWN + (w + 1) * 512],
                                                   vtmp[0:1, :], ALU.subtract),
                    reads=["rows", "vtmp"], writes=["rows"])
            s.s(lambda e: e.activation(rows[0:1, OWN:2 * OWN], rows[0:1, OWN:2 * OWN],
                                       AF.Ln, bias=consts[0:1, 0:1]),
                reads=["rows", "consts"], writes=["rows"])
            s.s(lambda e: e.activation(rows[0:1, OWN:2 * OWN], rows[0:1, OWN:2 * OWN],
                                       AF.Exp, scale=-0.5),
                reads=["rows"], writes=["rows"])
            for w in range(OWN // 512):
                s.t(lambda e, w=w: e.matmul(psB[:, :], onesf[0:1, :],
                                            rows[0:1, w * 512:(w + 1) * 512],
                                            start=True, stop=True),
                    reads=["rows", "onesf"], writes=["cbpsB"])
                s.s(lambda e, w=w: e.activation(mubc[:, w * 512:(w + 1) * 512], psB[:, :],
                                                AF.Identity),
                    reads=["cbpsB"], writes=["mubc"])
                s.t(lambda e, w=w: e.matmul(psB[:, :], onesf[0:1, :],
                                            rows[0:1, OWN + w * 512:OWN + (w + 1) * 512],
                                            start=True, stop=True),
                    reads=["rows", "onesf"], writes=["cbpsB"])
                s.s(lambda e, w=w: e.activation(rsbc[:, w * 512:(w + 1) * 512], psB[:, :],
                                                AF.Identity),
                    reads=["cbpsB"], writes=["rsbc"])
            for half in range(2):
                s.t(lambda e, half=half: e.matmul(psB[:, :], onesf[0:1, :],
                                                  b2s[0:1, half * 512:(half + 1) * 512],
                                                  start=True, stop=True),
                    reads=["b2s", "onesf"], writes=["cbpsB"])
                s.s(lambda e, half=half: e.activation(b2bc[:, half * 512:(half + 1) * 512],
                                                      psB[:, :], AF.Identity),
                    reads=["cbpsB"], writes=["b2bc"])
            r0 = 0
            for m in range(22):
                s.dma("gpsimd", w2b[m][:, :], w2i[r0:r0 + MCH[m], :], writes=[f"w2b{m}"])
                r0 += MCH[m]
            HW = OWN // 4
            wins_h = [(w * 512, min(512, HW - w * 512)) for w in range((HW + 511) // 512)]
            for th in range(4):
                tc0 = th * HW
                for d in range(8):
                    off = 128 + tc0 - (1 if d >= 4 else 0)
                    for (wc, wl) in wins_h:
                        s.v(lambda e, d=d, off=off, wc=wc, wl=wl, tc0=tc0: e.tensor_tensor(
                            t1f[:, 0:wl], zT[d][:, off + wc:off + wc + wl],
                            mubc[:, tc0 + wc:tc0 + wc + wl], ALU.subtract),
                            reads=[f"zT{d}", "mubc"], writes=["t1f"])
                        s.v(lambda e, wc=wc, wl=wl, tc0=tc0: e.tensor_tensor(
                            t1f[:, 0:wl], t1f[:, 0:wl], rsbc[:, tc0 + wc:tc0 + wc + wl],
                            ALU.mult),
                            reads=["t1f", "rsbc"], writes=["t1f"])
                        s.v(lambda e, d=d, wc=wc, wl=wl: e.tensor_scalar(
                            xnT[d][:, wc:wc + wl], t1f[:, 0:wl],
                            lngs[:, d:d + 1], lnbs[:, d:d + 1], ALU.mult, ALU.add),
                            reads=["t1f", "lngs", "lnbs"], writes=[f"xnT{d}"])
                c0a = 0
                for m in range(22):
                    mw = MCH[m]
                    bm = m % 2
                    W1A, W1G = f"w1sa{bm}", f"w1sg{bm}"
                    for d in range(8):
                        s.dma("gpsimd", w1sa[bm][:, d * 128:d * 128 + mw],
                              w1i[d * 128:(d + 1) * 128, c0a:c0a + mw], writes=[W1A])
                        s.dma("gpsimd", w1sg[bm][:, d * 128:d * 128 + mw],
                              w1i[d * 128:(d + 1) * 128, DIN + c0a:DIN + c0a + mw], writes=[W1G])
                    for (wc, wl) in wins_h:
                        PH, PG = "cbpsH0", "cbpsH1"
                        for d in range(8):
                            s.t(lambda e, d=d, wc=wc, wl=wl, bm=bm, mw=mw: e.matmul(
                                psH[0][0:mw, 0:wl], w1sa[bm][:, d * 128:d * 128 + mw],
                                xnT[d][:, wc:wc + wl],
                                start=(d == 0), stop=(d == 7)),
                                reads=[W1A, f"xnT{d}"], writes=[PH])
                        for d in range(8):
                            s.t(lambda e, d=d, wc=wc, wl=wl, bm=bm, mw=mw: e.matmul(
                                psH[1][0:mw, 0:wl], w1sg[bm][:, d * 128:d * 128 + mw],
                                xnT[d][:, wc:wc + wl],
                                start=(d == 0), stop=(d == 7)),
                                reads=[W1G, f"xnT{d}"], writes=[PG])
                        s.s(lambda e, m=m, wc=wc, wl=wl, mw=mw: e.activation(
                            ga[0:mw, wc:wc + wl], psH[0][0:mw, 0:wl], AF.Identity,
                            bias=b1s[0:mw, m:m + 1]),
                            reads=[PH, "b1s"], writes=["ga"])
                        s.s(lambda e, m=m, wc=wc, wl=wl, mw=mw: e.activation(
                            gg[0:mw, wc:wc + wl], psH[1][0:mw, 0:wl], AF.Gelu,
                            bias=b1s[0:mw, m + 22:m + 23]),
                            reads=[PG, "b1s"], writes=["gg"])
                    s.v(lambda e, m=m, mw=mw: e.tensor_tensor(
                        actT[m][0:mw, 0:HW], ga[0:mw, 0:HW], gg[0:mw, 0:HW],
                        ALU.mult),
                        reads=["ga", "gg"], writes=[f"actT{m}"])
                    c0a += mw
                for ii in range(HW // 128):
                    i = th * (HW // 128) + ii
                    bi = i % 2
                    s.dma("sync", zres[:, :], z1in[(1 + i) * 128:(2 + i) * 128, :],
                          writes=["zres"])
                    if final:
                        s.dma("sync", x1l[:, :], x1own[i * 128:(i + 1) * 128, :], writes=["x1l"])
                        s.dma("sync", s2l[:, :], s2own[i * 128:(i + 1) * 128, :], writes=["s2l"])
                    for nb in range(2):
                        bn = nb
                        PY = f"cbpsY{bn}"
                        for m in range(22):
                            s.t(lambda e, m=m, ii=ii, nb=nb, bn=bn, mw=MCH[m]: e.matmul(
                                psY[bn][:, :], actT[m][0:mw, ii * 128:(ii + 1) * 128],
                                w2b[m][0:mw, nb * 512:(nb + 1) * 512],
                                start=(m == 0), stop=(m == 21)),
                                reads=[f"actT{m}", f"w2b{m}"], writes=[PY])
                        s.v(lambda e, nb=nb, bn=bn: e.tensor_tensor(
                            z2s[:, nb * 512:(nb + 1) * 512], psY[bn][:, :],
                            b2bc[:, nb * 512:(nb + 1) * 512], ALU.add),
                            reads=[PY, "b2bc"], writes=["z2s"])
                        s.v(lambda e, nb=nb: e.tensor_tensor(
                            z2s[:, nb * 512:(nb + 1) * 512], z2s[:, nb * 512:(nb + 1) * 512],
                            zres[:, nb * 512:(nb + 1) * 512], ALU.add),
                            reads=["z2s", "zres"], writes=["z2s"])
                    if final:
                        s.s(lambda e: e.activation(z2s[:, :], z2s[:, :], AF.Silu),
                            reads=["z2s"], writes=["z2s"])
                        s.v(lambda e: e.tensor_tensor(z2s[:, :], z2s[:, :], s2l[:, :], ALU.mult),
                            reads=["z2s", "s2l"], writes=["z2s"])
                        s.v(lambda e: e.tensor_tensor(z2s[:, :], z2s[:, :], x1l[:, :], ALU.add),
                            reads=["z2s", "x1l"], writes=["z2s"])
                    s.dma("scalar", z2o[i * 128:(i + 1) * 128, :], z2s[:, :],
                          reads=["z2s"], writes=[f"hbm_z2_{i}"])
        for qe in ("sync", "gpsimd", "scalar"):
            s.dma_final_wait(qe)
        with nc.Block() as block:
            s.flush(block)
    return nc


def rotary_tables_np(SEQ):
    half = np.arange(0, 64, 2, dtype=np.float64)
    inv_freq = 1.0 / 10000.0 ** (half / 64)
    base_scale = (half + 0.4 * 64) / (1.4 * 64)
    def at(t):
        freqs = t[:, None] * inv_freq[None, :]
        freqs = np.concatenate([freqs, freqs], -1)
        power = (t - SEQ // 2) / 128.0
        scale = base_scale[None, :] ** power[:, None]
        scale = np.concatenate([scale, scale], -1)
        return freqs, scale
    return at


def prep_Ca(inputs, i, z, cfg=CFG):
    import ml_dtypes
    SEQ, OWN = cfg.SEQ, cfg.OWN
    Wqkv = np.asarray(inputs["L1_ffn_Wqkv"], np.float32)[i]
    Wo = np.asarray(inputs["L1_ffn_Wo"], np.float32)[i]
    qs = np.asarray(inputs["L1_ffn_q_scale"], np.float32)[i].reshape(64, 1)
    ks = np.asarray(inputs["L1_ffn_k_scale"], np.float32)[i].reshape(64, 1)
    wqkvp = np.zeros((16, 8, 128, 192), np.float32)
    for h in range(16):
        for d in range(8):
            r = slice(d * 128, (d + 1) * 128)
            wqkvp[h, d, :, 0:64] = Wqkv[r, h * 64:(h + 1) * 64]
            wqkvp[h, d, :, 64:128] = Wqkv[r, 1024 + h * 64:1024 + (h + 1) * 64]
            wqkvp[h, d, :, 128:192] = Wqkv[r, 2048 + h * 64:2048 + (h + 1) * 64]
    tab = rotary_tables_np(SEQ)
    maps = []
    for c in range(8):
        b, j = c // cfg.GROUP, c % cfg.GROUP
        s0 = j * OWN
        if s0 - CA_HALO < 0:
            zin = np.concatenate([np.zeros((CA_HALO - s0, 1024), np.float32),
                                  z[b][0:s0 + OWN]], 0)
        else:
            zin = z[b][s0 - CA_HALO:s0 + OWN]
        t = np.arange(s0 - CA_HALO, s0 + OWN, dtype=np.float64)
        fr, sc = tab(t)
        bf = ml_dtypes.bfloat16
        csq = np.ascontiguousarray((np.cos(fr) * sc * 8.0).T.astype(bf))
        ssq = np.ascontiguousarray((np.sin(fr) * sc * 8.0).T.astype(bf))
        csk = np.ascontiguousarray((np.cos(fr) / sc).T.astype(bf))
        ssk = np.ascontiguousarray((np.sin(fr) / sc).T.astype(bf))
        NW = OWN // 256
        masks = np.zeros((NW, 2, 128, 512), np.float32)
        xi = np.arange(512)[None, :]
        for w in range(NW):
            aw = (s0 + 256 * w) // 256
            for qt in range(2):
                rq = (qt * 128 + np.arange(128))[:, None]
                bad = (xi < rq) | (xi > rq + 256)
                if aw == 0:
                    bad |= (xi < 256)
                masks[w, qt][bad] = NEG
        maps.append(dict(zin=np.ascontiguousarray(zin), wqkvp=wqkvp, woi=Wo,
                         qsc=qs, ksc=ks, csq=csq, ssq=ssq, csk=csk, ssk=ssk,
                         masks=masks.astype(bf)))
    return maps


def prep_Cb(inputs, i, z1, cfg=CFG, final=False, x1=None, s2=None):
    import ml_dtypes
    OWN = cfg.OWN
    W1 = np.asarray(inputs["L1_ffn_W1"], np.float32)[i]
    W2 = np.asarray(inputs["L1_ffn_W2"], np.float32)[i]
    b1 = np.asarray(inputs["L1_ffn_b1"], np.float32)[i]
    b2 = np.asarray(inputs["L1_ffn_b2"], np.float32)[i]
    lng = np.asarray(inputs["L1_ffn_ln_g"], np.float32)[i]
    lnb = np.asarray(inputs["L1_ffn_ln_b"], np.float32)[i]
    b1c = np.zeros((128, 44), np.float32)
    c0 = 0
    for m in range(22):
        mw = MCH[m]
        b1c[0:mw, m] = b1[c0:c0 + mw]
        b1c[0:mw, m + 22] = b1[DIN + c0:DIN + c0 + mw]
        c0 += mw
    common = dict(w1i=W1, w2i=W2, b1ci=b1c, b2i=b2.reshape(1, -1),
                  lngc=np.ascontiguousarray(lng.reshape(8, 128).T),
                  lnbc=np.ascontiguousarray(lnb.reshape(8, 128).T))
    maps = []
    for c in range(8):
        b, j = c // cfg.GROUP, c % cfg.GROUP
        s0 = j * OWN
        if s0 - 128 < 0:
            z1in = np.concatenate([np.zeros((128, 1024), np.float32), z1[b][0:OWN]], 0)
        else:
            z1in = z1[b][s0 - 128:s0 + OWN]
        m = dict(common, z1in=np.ascontiguousarray(z1in))
        if final:
            m["x1own"] = np.ascontiguousarray(x1[b][s0:s0 + OWN])
            m["s2own"] = np.ascontiguousarray(s2[b][s0:s0 + OWN].astype(ml_dtypes.bfloat16))
        maps.append(m)
    return maps


def assemble_own(results, key, cfg=CFG, dtype=np.float32):
    out = np.zeros((2, cfg.SEQ, cfg.DM), dtype)
    for c in range(8):
        b, j = c // cfg.GROUP, c % cfg.GROUP
        out[b, j * cfg.OWN:(j + 1) * cfg.OWN] = np.asarray(results[c][key]).astype(dtype)
    return out


_PROGS = {}


def _prog(name, builder):
    if name not in _PROGS:
        _PROGS[name] = builder()
    return _PROGS[name]


def kernel(**inputs):
    cfg = CFG
    ncA = _prog("A", build_launch_A)
    in_maps, assemble = prep_A(inputs, cfg)
    res = run_bass_kernel_spmd(ncA, in_maps, core_ids=list(range(8)))
    x1 = assemble(res.results)
    ncB1 = _prog("B1", build_launch_B1)
    ncB2 = _prog("B2", build_launch_B2)
    b1_maps, b2_maps_fn, assemble_s2, xn2f_asm = prep_B(inputs, x1, cfg)
    res1 = run_bass_kernel_spmd(ncB1, b1_maps, core_ids=list(range(8)))
    maps2 = b2_maps_fn(res1.results)
    res2 = run_bass_kernel_spmd(ncB2, maps2, core_ids=list(range(8)))
    s2 = assemble_s2(res2.results)
    z = xn2f_asm(res1.results)
    ncCa = _prog("Ca", build_launch_Ca)
    ncCb = _prog("Cb", build_launch_Cb)
    ncCbF = _prog("CbF", lambda: build_launch_Cb(final=True))
    for i in range(6):
        mapsA = prep_Ca(inputs, i, z, cfg)
        resA = run_bass_kernel_spmd(ncCa, mapsA, core_ids=list(range(8)))
        z1 = assemble_own(resA.results, "z1o", cfg)
        fin = (i == 5)
        mapsB = prep_Cb(inputs, i, z1, cfg, final=fin,
                        x1=x1 if fin else None, s2=s2 if fin else None)
        resB = run_bass_kernel_spmd(ncCbF if fin else ncCb, mapsB, core_ids=list(range(8)))
        z = assemble_own(resB.results, "z2o", cfg)
    return z
